# version 14
# baseline (speedup 1.0000x reference)
import numpy as np
from contextlib import ExitStack

import concourse.bass as bass
import concourse.mybir as mybir
from concourse.bass_utils import run_bass_kernel_spmd

F32 = mybir.dt.float32
BF16 = mybir.dt.bfloat16
AF = mybir.ActivationFunctionType
ALU = mybir.AluOpType
AX = mybir.AxisListType

NT = 4096
D = 1024
H = 16
DH = 64
FF = 4096
NQG = 8
NKT = 32
TQ = 1024
ENGINES = ("sync", "gpsimd", "tensor", "vector", "scalar")
CSEM = {"sync": "dsy", "gpsimd": "dg", "tensor": "t", "vector": "v", "scalar": "s"}
DEBUG = False


class Buf:

    def __init__(self, ap):
        self.ap = ap
        self.w = None
        self.r = {}

    def rtoks(self):
        return list(self.r.values())


class Prog:
    def __init__(self):
        self.q = {e: [] for e in ENGINES}
        self.plan = {}
        self.handles = {}
        self.waited = {e: {} for e in ENGINES}
        self.floor = []

    def sem(self, nc, ctx, name):
        self.plan[name] = 0
        self.handles[name] = ctx.enter_context(nc.semaphore(name))

    def set_floor(self):
        self.floor = [(s, v, None, True) for s, v in self.plan.items() if v > 0]

    def _emit(self, eng, fn, waits, inc, dma):
        ws = tuple(w for w in waits if w is not None) + tuple(self.floor)
        tok = None
        if inc is not None:
            self.plan[inc] += 16 if dma else 1
            tok = (inc, self.plan[inc], eng, dma)
        self.q[eng].append((fn, ws, tok))
        return tok

    def op(self, eng, fn, reads=(), writes=(), extra=(), dma=False, inc=True):
        waits = []
        for b in reads:
            waits.append(b.w)
        for b in writes:
            waits.append(b.w)
            waits.extend(b.rtoks())
        waits.extend(extra)
        sem = (("dsy" if eng == "sync" else ("dsc" if eng == "scalar" else "dg")) if dma else
               ("g" if eng == "gpsimd" else CSEM[eng]))
        tok = self._emit(eng, fn, waits, sem if inc else None, dma)
        if tok is not None:
            for b in reads:
                b.r[tok[0]] = tok
            for b in writes:
                b.w = tok
                b.r = {}
        return tok

    def mm(self, bank, out_ap, pairs, reads=(), extra=(), start=True, stop=True):
        waits = ([bank.w] if not start else [bank.w] + bank.rtoks()) + \
            [b.w for b in reads] + list(extra)
        n = len(pairs)
        tok = None
        for i, (l, r) in enumerate(pairs):
            st = start and i == 0
            sp = stop and i == n - 1
            tok = self._emit(
                "tensor",
                (lambda E, o=out_ap, l=l, r=r, st=st, sp=sp:
                 E.matmul(o, l, r, start=st, stop=sp)),
                waits if i == 0 else (), "t" if i == n - 1 else None, False)
        bank.w = tok
        if stop:
            bank.r = {}
        for b in reads:
            b.r["t"] = tok
        return tok

    def flush(self, block):
        for eng in ENGINES:
            items = self.q[eng]
            if not items:
                continue
            self.q[eng] = []

            def body(E, items=items, eng=eng):
                waited = self.waited[eng]
                for fn, ws, tok in items:
                    for (sem, val, peng, isdma) in ws:
                        if peng == eng and not isdma and eng == "tensor":
                            continue
                        if waited.get(sem, 0) >= val:
                            continue
                        E.wait_ge(self.handles[sem], val)
                        waited[sem] = val
                    inst = fn(E)
                    if tok is not None:
                        inst.then_inc(self.handles[tok[0]], 16 if tok[3] else 1)
            getattr(block, eng)(body)


def build_nc():
    nc = bass.Bass("TRN2", target_bir_lowering=False)
    dt = lambda n, s, k: nc.dram_tensor(n, s, F32, kind=k).ap()
    x_d = dt("x", [NT, D], "ExternalInput")
    wq_d = dt("wq", [D, D], "ExternalInput")
    wkv_d = dt("wkv", [D, 2 * DH], "ExternalInput")
    wo_d = dt("wo", [D, D], "ExternalInput")
    g1_d = dt("g1t", [128, 8], "ExternalInput")
    b1_d = dt("b1t", [128, 8], "ExternalInput")
    g2_d = dt("g2r", [128, D], "ExternalInput")
    b2_d = dt("b2r", [128, D], "ExternalInput")
    w1_d = dt("w1t", [D, FF], "ExternalInput")
    w2_d = dt("w2t", [D, FF], "ExternalInput")
    h1_d = dt("h1t", [FF, FF], "ExternalInput")
    h2_d = dt("h2t", [FF, FF], "ExternalInput")
    w3_d = dt("w3t", [FF, D], "ExternalInput")
    s1_d = dt("s1t", [128, 32], "ExternalInput")
    s2_d = dt("s2t", [128, 32], "ExternalInput")
    h1s_d = dt("h1st", [128, 32], "ExternalInput")
    h2s_d = dt("h2st", [128, 32], "ExternalInput")
    s3_d = dt("s3r", [128, D], "ExternalInput")
    y_d = dt("y", [NT, D], "ExternalOutput")
    c_d = dt("c_scr", [NQG * 64, 128], "ExternalOutput" if DEBUG else "Internal")
    h1bf_d = nc.dram_tensor("h1bf_scr", [32, 128, 32 * 128], BF16, kind="Internal").ap()
    h2bf_d = nc.dram_tensor("h2bf_scr", [32, 128, 32 * 128], BF16, kind="Internal").ap()
    w3bf_d = nc.dram_tensor("w3bf_scr", [2, 8, 128, 2048], BF16, kind="Internal").ap()
    xp_d = dt("xp_scr", [NT, D], "ExternalOutput" if DEBUG else "Internal")
    if DEBUG:
        xn_dbg = dt("xn_dbg", [D, NT], "ExternalOutput")
        qh_dbg = dt("qh_dbg", [D, NT], "ExternalOutput")
        kx_dbg = dt("kx_dbg", [128, NT], "ExternalOutput")
        vx_dbg = dt("vx_dbg", [128, NKT * 65], "ExternalOutput")
        o_dbg = dt("o_dbg", [D, NT], "ExternalOutput")

    P = Prog()
    with ExitStack() as top:
        et = top.enter_context
        for sname in ("dsy", "dg", "dsc", "t", "v", "s", "g"):
            P.sem(nc, top, sname)

        SB = lambda n, s, d: Buf(et(nc.sbuf_tensor(n, s, d)))
        g1t = SB("g1t_sb", [128, 8], F32)
        b1t = SB("b1t_sb", [128, 8], F32)
        G2r = SB("g2r_sb", [128, D], F32)
        B2r = SB("b2r_sb", [128, D], F32)
        s1t = SB("s1t_sb", [128, 32], F32)
        s2t = SB("s2t_sb", [128, 32], F32)
        h1st = SB("h1st_sb", [128, 32], F32)
        h2st = SB("h2st_sb", [128, 32], F32)
        epsT = SB("epsT", [128, 1], F32)
        nhalfT = SB("nhalfT", [128, 1], F32)
        neg1T = SB("neg1T", [128, 1], F32)
        eighthT = SB("eighthT", [128, 1], F32)
        halfT = SB("halfT", [128, 1], F32)
        quartT = SB("quartT", [128, 1], F32)
        identb = SB("identb", [128, 128], BF16)
        identf = SB("identf", [128, 128], F32)
        ones64 = SB("ones64", [1, 64], F32)
        lnsets = []
        for i in range(2):
            lnsets.append(dict(
                st=SB(f"lnst{i}", [128, 24], F32),
                mv=SB(f"lnmv{i}", [128, 8], F32),
                aux=SB(f"lnax{i}", [128, 12], F32),
                lnv=SB(f"lnlv{i}", [128, 2], F32),
                rstd=SB(f"lnrs{i}", [128, 2], F32),
                negmr=SB(f"lnnm{i}", [128, 2], F32)))
        ln_rot = [0]
        psPa = et(nc.psum_tensor("psPa", [128, 1024], F32))
        psPb = et(nc.psum_tensor("psPb", [128, 1024], F32))
        pA0, pA1, pB0, pB1 = Buf(psPa), Buf(psPa), Buf(psPb), Buf(psPb)
        pb4 = Buf(et(nc.psum_tensor("pb4", [128, 512], F32)))
        pb5 = Buf(et(nc.psum_tensor("pb5", [128, 512], F32)))
        pb6 = Buf(et(nc.psum_tensor("pb6", [128, 512], F32)))
        pbt = [Buf(et(nc.psum_tensor("pbt0", [128, 1024], BF16)))]
        halves = [(pA0, psPa, 0), (pA1, psPa, 512), (pB0, psPb, 0), (pB1, psPb, 512)]
        cbuf = Buf(c_d)
        xpbuf = Buf(xp_d)

        block = et(nc.Block())

        def ln_pair(jobs):
            n = len(jobs)
            S = lnsets[ln_rot[0]]
            ln_rot[0] ^= 1
            st, mv, aux = S["st"], S["mv"], S["aux"]
            for s, (sb, _, _) in enumerate(jobs):
                P.op("vector", lambda E, a=st.ap, b=sb.ap, s=s: E.bn_stats(a[:, s * 12:s * 12 + 6], b[:, 0:512]),
                     reads=[sb], writes=[st])
                P.op("vector", lambda E, a=st.ap, b=sb.ap, s=s: E.bn_stats(a[:, s * 12 + 6:s * 12 + 12], b[:, 512:1024]),
                     reads=[sb], writes=[st])
            for s in range(n):
                P.op("vector", lambda E, a=mv.ap, b=st.ap, s=s: E.bn_aggr(a[:, s * 4:s * 4 + 2], b[:, s * 12:s * 12 + 6]),
                     reads=[st], writes=[mv])
                P.op("vector", lambda E, a=mv.ap, b=st.ap, s=s: E.bn_aggr(a[:, s * 4 + 2:s * 4 + 4], b[:, s * 12 + 6:s * 12 + 12]),
                     reads=[st], writes=[mv])
            mvv = lambda k0, k1: mv.ap[:, 0:4 * n].rearrange("p (s k) -> p s k", s=n)[:, :, k0:k1]
            P.op("vector", lambda E, a=aux.ap, n=n: E.tensor_add(
                a[:, 0:2 * n].rearrange("p (s k) -> p s k", s=n), mvv(0, 2), mvv(2, 4)),
                reads=[mv], writes=[aux])
            auxv = lambda c: aux.ap[:, 0:2 * n].rearrange("p (s k) -> p s k", s=n)[:, :, c:c + 1]
            P.op("vector", lambda E, a=aux.ap, n=n: E.tensor_sub(
                a[:, 4:4 + n].rearrange("p (s o) -> p s o", o=1), mvv(0, 1), mvv(2, 3)),
                reads=[mv], writes=[aux])
            P.op("vector", lambda E, a=aux.ap, n=n: E.tensor_mul(a[:, 6:6 + n], a[:, 4:4 + n], a[:, 4:4 + n]),
                 writes=[aux])
            P.op("vector", lambda E, a=aux.ap, n=n: E.tensor_scalar(
                a[:, 8:8 + n].rearrange("p (s o) -> p s o", o=1), auxv(1), halfT.ap[:, 0:1], None, ALU.mult),
                reads=[halfT], writes=[aux])
            P.op("vector", lambda E, a=aux.ap, n=n: E.scalar_tensor_tensor(
                a[:, 10:10 + n], a[:, 6:6 + n], quartT.ap[:, 0:1], a[:, 8:8 + n], ALU.mult, ALU.add),
                reads=[quartT], writes=[aux])
            P.op("scalar", lambda E, a=S["lnv"].ap, b=aux.ap, n=n: E.activation(
                a[:, 0:n], b[:, 10:10 + n], AF.Ln, bias=epsT.ap[:, :], scale=1.0),
                reads=[aux, epsT], writes=[S["lnv"]])
            P.op("scalar", lambda E, a=S["rstd"].ap, b=S["lnv"].ap, n=n: E.activation(
                a[:, 0:n], b[:, 0:n], AF.Exp, bias=0.0, scale=nhalfT.ap[:, :]),
                reads=[S["lnv"], nhalfT], writes=[S["rstd"]])
            P.op("vector", lambda E, a=S["negmr"].ap, b=S["rstd"].ap, n=n: E.scalar_tensor_tensor(
                a[:, 0:n], auxv(0).rearrange("p s o -> p (s o)") if False else aux.ap[:, 0:2 * n].rearrange("p (s k) -> p s k", s=n)[:, :, 0:1],
                nhalfT.ap[:, 0:1], b[:, 0:n], ALU.mult, ALU.mult),
                reads=[aux, S["rstd"], nhalfT], writes=[S["negmr"]])
            toks = []
            for s, (sb, dap, dbuf) in enumerate(jobs):
                toks.append(P.op("vector", lambda E, a=dap, b=sb.ap, c=S["rstd"].ap, d=S["negmr"].ap, s=s: E.tensor_scalar(
                    a, b[:, :], c[:, s:s + 1], d[:, s:s + 1], ALU.mult, ALU.add),
                    reads=[sb, S["rstd"], S["negmr"]], writes=[dbuf]))
            return toks

        def ln_chain(src, dst_ap, dst_buf, extra_reads=()):
            return ln_pair([(src, dst_ap, dst_buf)])[0]

        for dst, src in ((g1t, g1_d), (b1t, b1_d), (G2r, g2_d), (B2r, b2_d),
                         (s1t, s1_d), (s2t, s2_d), (h1st, h1s_d), (h2st, h2s_d)):
            P.op("sync", lambda E, a=dst.ap, b=src: E.dma_start(a[:, :], b), writes=[dst], dma=True)
        for cb, val in ((epsT, 1e-5), (nhalfT, -0.5), (neg1T, -1.0), (eighthT, 0.125), (halfT, 0.5), (quartT, 0.25)):
            P.op("vector", lambda E, a=cb.ap, v=val: E.memset(a[:, :], v), writes=[cb])
        P.op("vector", lambda E: E.memset(ones64.ap[:, :], 1.0), writes=[ones64])
        P.op("gpsimd", lambda E: E.memset(identb.ap[:, :], 0.0), writes=[identb])
        P.op("gpsimd", lambda E: E.affine_select(out=identb.ap[:, :], in_=identb.ap[:, :], compare_op=ALU.not_equal,
                                                 fill=1.0, base=0, pattern=[[-1, 128]], channel_multiplier=1),
             writes=[identb])
        P.op("gpsimd", lambda E: E.memset(identf.ap[:, :], 0.0), writes=[identf])
        P.op("gpsimd", lambda E: E.affine_select(out=identf.ap[:, :], in_=identf.ap[:, :], compare_op=ALU.not_equal,
                                                 fill=1.0, base=0, pattern=[[-1, 128]], channel_multiplier=1),
             writes=[identf])

        pa = ExitStack()
        ep = pa.enter_context
        SA = lambda n, s, d: Buf(ep(nc.sbuf_tensor(n, s, d)))
        wq_sb = SA("wq_sb", [128, 8 * 1024], BF16)
        wkv_sb = SA("wkv_sb", [128, 8 * 128], BF16)
        qhat = [SA(f"qh{c}", [128, NT], BF16) for c in range(8)]
        kext = SA("kext", [128, NT], BF16)
        vext = SA("vext", [128, NKT * 65], BF16)
        ofm2 = [[SA(f"ofm{p}_{c}", [128, 512], BF16) for c in range(8)] for p in range(2)]
        lnA = [SA(f"lnA{i}", [128, D], F32) for i in range(2)]
        xnb = [SA(f"xnb{i}", [128, D], BF16) for i in range(2)]
        xnfm2 = [[SA(f"xnfm{p}_{c}", [128, 512], BF16) for c in range(8)] for p in range(2)]
        vtmp = SA("vtmp", [64, 512], BF16)
        qext = [SA(f"qext{i}", [128, 512], BF16) for i in range(2)]
        ptb = [SA(f"ptb{i}", [128, 1024], BF16) for i in range(3)]
        calx = [SA(f"calx{i}", [128, 4], F32) for i in range(2)]
        ctbx = [SA(f"ctbx{i}", [4, 128], F32) for i in range(2)]
        rrow = SA("rrow", [1, 512], F32)
        Rsb = SA("Rsb", [64, 512], F32)
        a_sb = [SA(f"a_sb{i}", [128, D], F32) for i in range(2)]
        t1 = [SA(f"t1_{i}", [128, D], F32) for i in range(2)]
        t2 = SA("t2", [128, D], F32)
        t3 = [SA(f"t3_{i}", [128, D], F32) for i in range(2)]
        xres = [SA(f"xr{i}", [128, D], F32) for i in range(2)]
        xtile = xres
        xp_sb = SA("xp_sb", [128, D], F32)
        hstg = SA("hstg", [128, 16 * 128], BF16)
        h1bf = Buf(h1bf_d)
        h2bf = Buf(h2bf_d)

        w3bf = Buf(w3bf_d)
        precast_jobs = [("w3", w3_d, w3bf_d, w3bf, j) for j in range(16)]
        precast_out = [None]

        def precast_step():
            if precast_out[0] is not None:
                kind, dstd, dbuf, j = precast_out[0]
                if kind == "h":
                    P.op("scalar", lambda E, d=dstd, j=j: E.dma_start(
                        d[j // 2][:, (j % 2) * 2048:(j % 2) * 2048 + 2048], hstg.ap[:, :]),
                        reads=[hstg], writes=[dbuf], dma=True)
                else:
                    P.op("scalar", lambda E, d=dstd, j=j: E.dma_start(
                        d[j // 8][j % 8], hstg.ap[:, :]),
                        reads=[hstg], writes=[dbuf], dma=True)
                precast_out[0] = None
            if precast_jobs:
                kind, wd, dstd, dbuf, j = precast_jobs.pop(0)
                old_toks = [hstg.w] + hstg.rtoks()
                if kind == "h":
                    oc, half = j // 2, j % 2
                    for kq in range(2):
                        tok = P.op("gpsimd", lambda E, wd=wd, oc=oc, half=half, kq=kq: E.dma_start(
                            hstg.ap[:, kq * 8 * 128:(kq + 1) * 8 * 128].rearrange("p (kc n) -> p kc n", kc=8),
                            wd[(half * 2 + kq) * 1024:(half * 2 + kq + 1) * 1024,
                               oc * 128:(oc + 1) * 128].rearrange("(kc p) n -> p kc n", p=128)),
                            extra=old_toks if kq == 0 else (), dma=True)
                else:
                    douth, sub = j // 8, j % 8
                    tok = P.op("gpsimd", lambda E, wd=wd, douth=douth, sub=sub: E.dma_start(
                        hstg.ap[:, :].rearrange("p (kc n) -> p kc n", kc=4),
                        wd[sub * 512:(sub + 1) * 512,
                           douth * 512:(douth + 1) * 512].rearrange("(kc p) n -> p kc n", p=128)),
                        extra=old_toks, dma=True)
                hstg.w = tok
                hstg.r = {}
                precast_out[0] = (kind, dstd, dbuf, j)

        P.op("gpsimd", lambda E: E.dma_start(
            wq_sb.ap[:, :].rearrange("p (kc n) -> p kc n", kc=8),
            wq_d.rearrange("(kc p) n -> p kc n", p=128)), writes=[wq_sb], dma=True)
        P.op("gpsimd", lambda E: E.dma_start(
            wkv_sb.ap[:, :].rearrange("p (kc n) -> p kc n", kc=8),
            wkv_d.rearrange("(kc p) n -> p kc n", p=128)), writes=[wkv_sb], dma=True)
        P.op("gpsimd", lambda E: E.memset(vext.ap[:, :], 1.0), writes=[vext])
        P.op("gpsimd", lambda E: E.memset(kext.ap[64:65, :], 1.0), writes=[kext])

        def emit_proj(qg, part=None):
            xnfm = xnfm2[qg % 2]
            ochs = range(8) if part is None else (range(4) if part == 0 else range(4, 8))
            for och in ochs:
                bank, bt, boff = halves[(och % 2) * 2]
                pairs = [(wq_sb.ap[:, kc * 1024 + och * 128: kc * 1024 + (och + 1) * 128], xnfm[kc].ap[:, 0:512])
                         for kc in range(8)]
                P.mm(bank, bt[:, boff:boff + 512], pairs, reads=[wq_sb] + xnfm)
                P.op("scalar", lambda E, o=qhat[och].ap, b=bt, boff=boff, qg=qg: E.activation(
                    o[:, qg * 512:(qg + 1) * 512], b[:, boff:boff + 512], AF.Copy, bias=0.0, scale=eighthT.ap[:, :]),
                    reads=[bank, eighthT], writes=[qhat[och]])
            if part == 0:
                return
            bank = pb4
            pairs = [(wkv_sb.ap[:, kc * 128:(kc + 1) * 128], xnfm[kc].ap[:, 0:512]) for kc in range(8)]
            P.mm(bank, bank.ap[:, :], pairs, reads=[wkv_sb] + xnfm)
            P.op("vector", lambda E, b=bank.ap, qg=qg: E.tensor_copy(kext.ap[0:64, qg * 512:(qg + 1) * 512], b[0:64, :]),
                 reads=[bank], writes=[kext])
            P.op("vector", lambda E, b=bank.ap: E.tensor_copy(vtmp.ap[:, :], b[64:128, :]),
                 reads=[bank], writes=[vtmp])
            for j in range(4):
                kt = qg * 4 + j
                tb = pbt[0]
                P.op("tensor", lambda E, o=tb.ap, j=j: E.transpose(o[:, 0:64], vtmp.ap[:, j * 128:(j + 1) * 128], identb.ap[0:64, 0:64]),
                     reads=[vtmp, identb], writes=[tb])
                P.op("vector", lambda E, o=vext.ap, i=tb.ap, kt=kt: E.tensor_copy(o[:, kt * 65:kt * 65 + 64], i[:, 0:64]),
                     reads=[tb], writes=[vext])

        pend_proj = [None]
        for qg in range(NQG):
            xnfm = xnfm2[qg % 2]
            for qp in range(2):
                tiles = [qg * 4 + 2 * qp, qg * 4 + 2 * qp + 1]
                for i, qglob in enumerate(tiles):
                    P.op("sync", lambda E, a=xtile[i].ap, q=qglob: E.dma_start(a[:, :], x_d[q * 128:(q + 1) * 128, :]),
                         writes=[xtile[i]], dma=True)
                ln_pair([(xtile[i], lnA[i].ap[:, :], lnA[i]) for i in range(2)])
                if pend_proj[0] is not None:
                    emit_proj(pend_proj[0], part=qp)
                    if qp == 1:
                        pend_proj[0] = None
                precast_step()
                ln_pair([(lnA[i], xnb[i].ap[:, :], xnb[i]) for i in range(2)])
                for i, qglob in enumerate(tiles):
                    qt = qglob % 4
                    for c in range(8):
                        tb = pbt[0]
                        P.op("tensor", lambda E, o=tb.ap, i2=xnb[i].ap, c=c: E.transpose(
                            o[:, 0:128], i2[:, c * 128:(c + 1) * 128], identb.ap[:, :]),
                            reads=[xnb[i], identb], writes=[tb])
                        P.op("vector", lambda E, o=xnfm[c].ap, i2=tb.ap, c=c, qt=qt: E.tensor_scalar(
                            o[:, qt * 128:(qt + 1) * 128], i2[:, 0:128], g1t.ap[:, c:c + 1], b1t.ap[:, c:c + 1], ALU.mult, ALU.add),
                            reads=[tb, g1t, b1t], writes=[xnfm[c]])
            pend_proj[0] = qg
        emit_proj(pend_proj[0])
        pend_proj[0] = None

        if DEBUG:
            for c in range(8):
                P.op("gpsimd", lambda E, c=c: E.dma_start(qh_dbg[c * 128:(c + 1) * 128, :], qhat[c].ap[:, :]),
                     reads=[qhat[c]], dma=True)
            P.op("gpsimd", lambda E: E.dma_start(kx_dbg, kext.ap[:, :]), reads=[kext], dma=True)
            P.op("gpsimd", lambda E: E.dma_start(vx_dbg, vext.ap[:, :]), reads=[vext], dma=True)

        P.op("gpsimd", lambda E: E.dma_start(
            wq_sb.ap[:, :].rearrange("p (kc n) -> p kc n", kc=8),
            wo_d.rearrange("(kc p) n -> p kc n", p=128)), writes=[wq_sb], dma=True)

        pend_ep = [None]
        pend_p4 = []
        cur_p4 = [None]

        def emit_ep_part(part):
            if pend_ep[0] is None:
                return
            h, eqg, o2 = pend_ep[0]
            ch, base = h // 2, (h % 2) * 64
            if part == 1:
                P.op("vector", lambda E, b=o2.ap: E.reciprocal(rrow.ap[0:1, :], b[64:65, :]),
                     reads=[o2], writes=[rrow])
            elif part == 2:
                P.mm(pb6, pb6.ap[0:64, :], [(ones64.ap[0:1, :], rrow.ap[0:1, :])], reads=[ones64, rrow])
                P.op("vector", lambda E, b=pb6.ap: E.tensor_copy(Rsb.ap[:, :], b[0:64, 0:512]),
                     reads=[pb6], writes=[Rsb])
            else:
                P.op("vector", lambda E, b=o2.ap, ch=ch, base=base, p=eqg % 2: E.tensor_mul(
                    ofm2[p][ch].ap[base:base + 64, :], b[0:64, :], Rsb.ap[:, :]),
                    reads=[o2, Rsb], writes=[ofm2[eqg % 2][ch]])
                pend_ep[0] = None

        p4_idx = [0]
        p4_prev = [None]

        def p4_half(qg, qt, half):
            ofm = ofm2[qg % 2]
            qglob = qg * 4 + qt
            p = p4_idx[0] % 2
            if half == 0:
                xrb = xres[qglob % 2]
                P.op("sync", lambda E, a=xrb.ap, q=qglob: E.dma_start(a[:, :], x_d[q * 128:(q + 1) * 128, :]),
                     writes=[xrb], dma=True)
            pairs = [(ofm[c].ap[:, qt * 128:(qt + 1) * 128],
                      wq_sb.ap[:, c * 1024 + half * 512: c * 1024 + (half + 1) * 512]) for c in range(8)]
            P.mm(pb6, pb6.ap[:, :], pairs, reads=ofm + [wq_sb])
            P.op("vector", lambda E, b=pb6.ap, half=half, p=p: E.tensor_copy(
                a_sb[p].ap[:, half * 512:(half + 1) * 512], b[:, :]), reads=[pb6], writes=[a_sb[p]])

        def p4_residual(qg, qt, p):
            qglob = qg * 4 + qt
            xrb = xres[qglob % 2]
            P.op("vector", lambda E, b=xrb.ap, t=t1[p].ap: E.tensor_add(xp_sb.ap[:, :], t[:, :], b[:, :]),
                 reads=[t1[p], xrb], writes=[xp_sb])
            P.op("sync", lambda E, q=qglob: E.dma_start(xp_d[q * 128:(q + 1) * 128, :], xp_sb.ap[:, :]),
                 reads=[xp_sb], writes=[xpbuf], dma=True)

        def p4_tail(qg, qt):
            p = p4_idx[0] % 2
            jobs = [(a_sb[p], t1[p].ap[:, :], t1[p])]
            if p4_prev[0] is not None:
                pq, pt_, pp = p4_prev[0]
                jobs.append((t3[pp], t1[pp].ap[:, :], t1[pp]))
            ln_pair(jobs)
            P.op("vector", lambda E, p=p: E.tensor_mul(t2.ap[:, :], t1[p].ap[:, :], G2r.ap[:, :]),
                 reads=[t1[p], G2r], writes=[t2])
            P.op("vector", lambda E, p=p: E.tensor_add(t3[p].ap[:, :], t2.ap[:, :], B2r.ap[:, :]),
                 reads=[t2, B2r], writes=[t3[p]])
            if p4_prev[0] is not None:
                p4_residual(*p4_prev[0])
            p4_prev[0] = (qg, qt, p)
            p4_idx[0] += 1

        def p4_flush():
            if p4_prev[0] is None:
                return
            pq, pt_, pp = p4_prev[0]
            ln_chain(t3[pp], t1[pp].ap[:, :], t1[pp])
            p4_residual(pq, pt_, pp)
            p4_prev[0] = None

        def emit_p4_job(qg, qt):
            p4_half(qg, qt, 0)
            p4_half(qg, qt, 1)
            p4_tail(qg, qt)

        def head_prep_a(qg, h):
            qx = qext[h % 2]
            P.op("gpsimd", lambda E, o=qx.ap, ch=h // 2, base=(h % 2) * 64, qg=qg: E.tensor_copy(
                o[0:64, :], qhat[ch].ap[base:base + 64, qg * 512:(qg + 1) * 512]),
                reads=[qhat[h // 2]], writes=[qx])
            for qt in range(4):
                P.mm(pb6, pb6.ap[:, qt * 128:(qt + 1) * 128],
                     [(qx.ap[0:64, qt * 128:(qt + 1) * 128], kext.ap[0:64, 0:128])],
                     reads=[qx, kext], start=(qt == 0), stop=(qt == 3))
            P.op("vector", lambda E, b=pb6.ap, c=calx[h % 2].ap: E.tensor_reduce(
                c[:, 0:4], b[:, :].rearrange("p (q n) -> p q n", q=4), AX.X, ALU.max, negate=True),
                reads=[pb6], writes=[calx[h % 2]])

        def head_prep_b(qg, h):
            qx = qext[h % 2]
            P.op("tensor", lambda E, o=pb6.ap, c=calx[h % 2].ap: E.transpose(o[0:4, 0:128], c[:, 0:4], identf.ap[:, :]),
                 reads=[calx[h % 2], identf], writes=[pb6])
            P.op("vector", lambda E, o=ctbx[h % 2].ap, i=pb6.ap: E.tensor_copy(o[:, :], i[0:4, 0:128]),
                 reads=[pb6], writes=[ctbx[h % 2]])
            P.op("gpsimd", lambda E, qg=qg, h=h, c=ctbx[h % 2].ap: E.dma_start(
                c_d[qg * 64 + h * 4: qg * 64 + (h + 1) * 4, :], c[:, :]),
                reads=[ctbx[h % 2]], writes=[cbuf], dma=True)
            P.op("gpsimd", lambda E, o=qx.ap, qg=qg, h=h: E.dma_start(
                o[64:65, 0:512], c_d[qg * 64 + h * 4: qg * 64 + (h + 1) * 4, :].rearrange("a b -> (a b)")[None, :]),
                reads=[cbuf], writes=[qx], dma=True)

        for qg in range(NQG):
            for h in range(H):
                if h == 0:
                    head_prep_a(qg, 0)
                    head_prep_b(qg, 0)
                if h == 1 and qg == 0:
                    head_prep_a(qg, 1)
                    head_prep_b(qg, 1)
                qx = qext[h % 2]
                o2 = pb4 if h % 2 == 0 else pb5
                prev = [None] * 16
                for kp in range(NKT // 2):
                    h0, h1 = (pA0, pA1) if kp % 2 == 0 else (pB0, pB1)
                    pst = psPa if kp % 2 == 0 else psPb
                    kt0, kt1 = 2 * kp, 2 * kp + 1
                    P.mm(h0, pst[:, 0:512],
                         [(kext.ap[0:65, kt0 * 128:(kt0 + 1) * 128], qx.ap[0:65, :])],
                         reads=[kext, qx])
                    P.mm(h1, pst[:, 512:1024],
                         [(kext.ap[0:65, kt1 * 128:(kt1 + 1) * 128], qx.ap[0:65, :])],
                         reads=[kext, qx])
                    tp = ptb[kp % 3]
                    P.op("scalar", lambda E, o=tp.ap, b=pst: E.activation(
                        o[:, :], b[:, :], AF.Exp, bias=0.0, scale=1.0),
                        reads=[h0, h1], writes=[tp])
                    prev[kp] = tp
                    if kp == 1:
                        emit_ep_part(1)
                    elif kp == 2:
                        emit_ep_part(2)
                    elif kp == 3:
                        emit_ep_part(3)
                    elif kp == 4 and h % 4 == 3 and pend_p4:
                        cur_p4[0] = pend_p4.pop(0)
                        p4_half(*cur_p4[0], 0)
                    elif kp == 6 and cur_p4[0] is not None:
                        p4_half(*cur_p4[0], 1)
                    elif kp == 8 and cur_p4[0] is not None:
                        p4_tail(*cur_p4[0])
                        cur_p4[0] = None
                    elif kp == 10:
                        nh, nqg = (h + 1, qg) if h + 1 < H else (0, qg + 1)
                        if not (nh == 0 or nqg >= NQG):
                            head_prep_a(nqg, nh)
                    elif kp == 12:
                        nh, nqg = (h + 1, qg) if h + 1 < H else (0, qg + 1)
                        if not (nh == 0 or nqg >= NQG):
                            head_prep_b(nqg, nh)
                    elif kp == 14:
                        precast_step()
                    if kp > 1:
                        pk = kp - 2
                        tpp = prev[pk]
                        P.mm(o2, o2.ap[0:65, :],
                             [(vext.ap[:, (2 * pk) * 65:(2 * pk + 1) * 65], tpp.ap[:, 0:512])],
                             reads=[vext, tpp], start=(kp == 2), stop=False)
                        P.mm(o2, o2.ap[0:65, :],
                             [(vext.ap[:, (2 * pk + 1) * 65:(2 * pk + 2) * 65], tpp.ap[:, 512:1024])],
                             reads=[vext, tpp], start=False, stop=False)
                for pk in (NKT // 2 - 2, NKT // 2 - 1):
                    tpp = prev[pk]
                    P.mm(o2, o2.ap[0:65, :],
                         [(vext.ap[:, (2 * pk) * 65:(2 * pk + 1) * 65], tpp.ap[:, 0:512])],
                         reads=[vext, tpp], start=False, stop=False)
                    P.mm(o2, o2.ap[0:65, :],
                         [(vext.ap[:, (2 * pk + 1) * 65:(2 * pk + 2) * 65], tpp.ap[:, 512:1024])],
                         reads=[vext, tpp], start=False, stop=(pk == NKT // 2 - 1))
                emit_ep_part(1), emit_ep_part(2), emit_ep_part(3)
                pend_ep[0] = (h, qg, o2)
            pend_p4.extend((qg, qt) for qt in range(4))

        emit_ep_part(1), emit_ep_part(2), emit_ep_part(3)
        while pend_p4:
            emit_p4_job(*pend_p4.pop(0))
        p4_flush()
        while precast_jobs or precast_out[0] is not None:
            precast_step()

        P.flush(block)
        pa.close()
        P.set_floor()

        p5 = ExitStack()
        e5 = p5.enter_context
        S5 = lambda n, s, d: Buf(e5(nc.sbuf_tensor(n, s, d)))
        u1s = [S5(f"u1s{c}", [128, TQ], BF16) for c in range(32)]
        w1c = [S5(f"w1c{i}", [128, 8 * 256], BF16) for i in range(2)]
        w2c = [S5("w2c0", [128, 8 * 256], BF16)]
        xp2 = [S5(f"xp2{i}", [128, D], F32) for i in range(2)]
        lnBb = [S5(f"lnBb{i}", [128, D], BF16) for i in range(2)]
        yln = None
        xr2 = [S5(f"xr2_{i}", [128, D], F32) for i in range(2)]
        lnBfm = [S5(f"lnBfm{c}", [128, TQ], BF16) for c in range(8)]
        lnBfmH = [[Buf(lnBfm[c].ap) for _ in range(2)] for c in range(8)]
        S3r = S5("s3r_sb", [128, D], F32)
        P.op("sync", lambda E: E.dma_start(S3r.ap[:, :], s3_d), writes=[S3r], dma=True)

        def emit_lnB_pair(Q, qp8):
            tiles = [Q * 8 + 2 * qp8, Q * 8 + 2 * qp8 + 1]
            for i, qglob in enumerate(tiles):
                P.op("sync", lambda E, a=xp2[i].ap, q=qglob: E.dma_start(a[:, :], xp_d[q * 128:(q + 1) * 128, :]),
                     reads=[xpbuf], writes=[xp2[i]], dma=True)
            ln_pair([(xp2[i], lnBb[i].ap[:, :], lnBb[i]) for i in range(2)])
            for i, qglob in enumerate(tiles):
                qt8 = qglob - Q * 8
                for c in range(8):
                    tb = pbt[0]
                    P.op("tensor", lambda E, o=tb.ap, i2=lnBb[i].ap, c=c: E.transpose(
                        o[:, 0:128], i2[:, c * 128:(c + 1) * 128], identb.ap[:, :]),
                        reads=[lnBb[i], identb], writes=[tb])
                    P.op("vector", lambda E, o=lnBfm[c].ap, i2=tb.ap, qt8=qt8: E.tensor_copy(
                        o[:, qt8 * 128:(qt8 + 1) * 128], i2[:, 0:128]),
                        reads=[tb], writes=[lnBfmH[c][qt8 // 4]])

        qgate = [[]]
        P.op("gpsimd", lambda E, a=w1c[0].ap: E.dma_start(
            a[:, :].rearrange("p (kc n) -> p kc n", kc=8),
            w1_d[:, 0:256].rearrange("(kc p) n -> p kc n", p=128)), writes=[w1c[0]], dma=True)
        P.op("gpsimd", lambda E, a=w2c[0].ap: E.dma_start(
            a[:, :].rearrange("p (kc n) -> p kc n", kc=8),
            w2_d[:, 0:256].rearrange("(kc p) n -> p kc n", p=128)), writes=[w2c[0]], dma=True)
        for qp8 in range(4):
            emit_lnB_pair(0, qp8)
        for Q in range(4):
            ch_ctx = ExitStack()
            eh = ch_ctx.enter_context
            Sh = lambda n, s, d: Buf(eh(nc.sbuf_tensor(f"{n}_q{Q}", s, d)))
            h1s = [Sh(f"h1s{c}", [128, TQ], BF16) for c in range(32)]
            h1c = [Sh(f"h1c{i}", [128, 32 * 128], BF16) for i in range(2)]
            for b in h1s + h1c:
                b.w = None
                b.r = {s_: t_ for s_, t_ in [(g[0], g) for g in qgate[0]]}

            for ocp in range(16):
                wb = w1c[ocp % 2]
                wb2 = w2c[0]
                P.op("gpsimd", lambda E, a=wb.ap, ocp=ocp: E.dma_start(
                    a[:, :].rearrange("p (kc n) -> p kc n", kc=8),
                    w1_d[:, ocp * 256:(ocp + 1) * 256].rearrange("(kc p) n -> p kc n", p=128)),
                    writes=[wb], dma=True)
                P.op("gpsimd", lambda E, a=wb2.ap, ocp=ocp: E.dma_start(
                    a[:, :].rearrange("p (kc n) -> p kc n", kc=8),
                    w2_d[:, ocp * 256:(ocp + 1) * 256].rearrange("(kc p) n -> p kc n", p=128)),
                    writes=[wb2], dma=True)
                for ocw in range(2):
                    oc = 2 * ocp + ocw
                    for th in range(2):
                        bank, bt, boff = halves[th]
                        pairs = [(wb.ap[:, kc * 256 + ocw * 128: kc * 256 + ocw * 128 + 128],
                                  lnBfm[kc].ap[:, th * 512:(th + 1) * 512]) for kc in range(8)]
                        P.mm(bank, bt[:, boff:boff + 512], pairs, reads=[wb] + [lnBfmH[kc][th] for kc in range(8)])
                        P.op("scalar", lambda E, o=u1s[oc].ap, b=bt, boff=boff, th=th, oc=oc: E.activation(
                            o[:, th * 512:(th + 1) * 512], b[:, boff:boff + 512], AF.Silu, bias=0.0, scale=s1t.ap[:, oc:oc + 1]),
                            reads=[bank, s1t], writes=[u1s[oc]])
                    for th in range(2):
                        bank, bt, boff = halves[2 + th]
                        pairs = [(wb2.ap[:, kc * 256 + ocw * 128: kc * 256 + ocw * 128 + 128],
                                  lnBfm[kc].ap[:, th * 512:(th + 1) * 512]) for kc in range(8)]
                        P.mm(bank, bt[:, boff:boff + 512], pairs, reads=[wb2] + [lnBfmH[kc][th] for kc in range(8)])
                        P.op("vector", lambda E, o=u1s[oc].ap, b=bt, boff=boff, th=th: E.tensor_mul(
                            o[:, th * 512:(th + 1) * 512], b[:, boff:boff + 512], o[:, th * 512:(th + 1) * 512]),
                            reads=[bank, u1s[oc]], writes=[u1s[oc]])
                        P.op("vector", lambda E, o=u1s[oc].ap, th=th, oc=oc: E.tensor_scalar(
                            o[:, th * 512:(th + 1) * 512], o[:, th * 512:(th + 1) * 512], s2t.ap[:, oc:oc + 1], None, ALU.mult),
                            reads=[s2t, u1s[oc]], writes=[u1s[oc]])

            for (wd, sct, src, dst, pb_off) in ((h1_d, h1st, u1s, h1s, 0), (h2_d, h2st, h1s, u1s, 2)):
                for oc in range(32):
                    wb = h1c[oc % 2]
                    old_toks = [wb.w] + wb.rtoks()
                    for kq in range(4):
                        tok = P.op("gpsimd", lambda E, a=wb.ap, oc=oc, kq=kq, wd=wd: E.dma_start(
                            a[:, kq * 8 * 128:(kq + 1) * 8 * 128].rearrange("p (kc n) -> p kc n", kc=8),
                            wd[kq * 1024:(kq + 1) * 1024, oc * 128:(oc + 1) * 128].rearrange("(kc p) n -> p kc n", p=128)),
                            extra=old_toks if kq == 0 else (), dma=True)
                    wb.w = tok
                    wb.r = {}
                    for th in range(2):
                        bank, bt, boff = halves[pb_off + th]
                        pairs = [(wb.ap[:, kc * 128:(kc + 1) * 128], src[kc].ap[:, th * 512:(th + 1) * 512])
                                 for kc in range(32)]
                        P.mm(bank, bt[:, boff:boff + 512], pairs, reads=[wb] + src)
                        P.op("scalar", lambda E, o=dst[oc].ap, b=bt, boff=boff, th=th, oc=oc, sct=sct: E.activation(
                            o[:, th * 512:(th + 1) * 512], b[:, boff:boff + 512], AF.Gelu, bias=0.0, scale=sct.ap[:, oc:oc + 1]),
                            reads=[bank, sct], writes=[dst[oc]])

            P.flush(block)
            ch_ctx.close()
            ygate = [("t", P.plan["t"], None, True), ("s", P.plan["s"], None, True)]

            cy_ctx = ExitStack()
            ey = cy_ctx.enter_context
            Sy = lambda n, s, d: Buf(ey(nc.sbuf_tensor(f"{n}_q{Q}", s, d)))
            w3h = Sy("w3h", [128, 32 * 512], BF16)
            w3hH = [Buf(w3h.ap) for _ in range(2)]
            y8 = [Sy(f"y8_{i}", [128, D], F32) for i in range(8)]
            for douth in range(2):
                for hh in range(2):
                    old_toks = [w3h.w] + w3hH[hh].rtoks() + (ygate if douth == 0 else [])
                    for sub in range(4 * hh, 4 * hh + 4):
                        tok = P.op("sync", lambda E, douth=douth, sub=sub: E.dma_start(
                            w3h.ap[:, sub * 2048:(sub + 1) * 2048], w3bf_d[douth][sub]),
                            reads=[w3bf], extra=old_toks if sub == 4 * hh else (), dma=True)
                    w3h.w = tok
                    w3hH[hh].w = tok
                    w3hH[hh].r = {}
                for qt8 in range(8):
                    bank, bt, boff = halves[(qt8 % 2) + (0 if douth == 0 else 2)]
                    for hh in range(2):
                        pairs = [(u1s[kc].ap[:, qt8 * 128:(qt8 + 1) * 128], w3h.ap[:, kc * 512:(kc + 1) * 512])
                                 for kc in range(16 * hh, 16 * hh + 16)]
                        P.mm(bank, bt[:, boff:boff + 512], pairs,
                             reads=[u1s[kc] for kc in range(16 * hh, 16 * hh + 16)] + [w3hH[hh]],
                             start=(hh == 0), stop=(hh == 1))
                    P.op("vector", lambda E, o=y8[qt8].ap, b=bt, boff=boff, douth=douth: E.scalar_tensor_tensor(
                        o[:, douth * 512:(douth + 1) * 512], b[:, boff:boff + 512], 1.0, S3r.ap[:, douth * 512:(douth + 1) * 512],
                        ALU.mult, ALU.mult),
                        reads=[bank, S3r], writes=[y8[qt8]],
                        extra=ygate if douth == 0 else ())
            for qp8 in range(4):
                tiles = [Q * 8 + 2 * qp8, Q * 8 + 2 * qp8 + 1]
                for i, qglob in enumerate(tiles):
                    P.op("sync", lambda E, a=xr2[i].ap, q=qglob: E.dma_start(a[:, :], xp_d[q * 128:(q + 1) * 128, :]),
                         reads=[xpbuf], writes=[xr2[i]], dma=True)
                ln_pair([(y8[t - Q * 8], y8[t - Q * 8].ap[:, :], y8[t - Q * 8]) for t in tiles])
                for i, qglob in enumerate(tiles):
                    yb = y8[qglob - Q * 8]
                    P.op("vector", lambda E, b=xr2[i].ap, y=yb.ap: E.tensor_add(b[:, :], y[:, :], b[:, :]),
                         reads=[yb], writes=[xr2[i]])
                    P.op("sync", lambda E, q=qglob, i=i: E.dma_start(y_d[q * 128:(q + 1) * 128, :], xr2[i].ap[:, :]),
                         reads=[xr2[i]], dma=True)
                if Q < 3:
                    emit_lnB_pair(Q + 1, qp8)
            P.flush(block)
            cy_ctx.close()
            qgate[0] = [("t", P.plan["t"], None, True), ("v", P.plan["v"], None, True),
                        ("s", P.plan["s"], None, True), ("dsy", P.plan["dsy"], None, True)]

        P._emit("sync", lambda E: E.nop(), [("dsy", P.plan["dsy"], None, True)], None, False)
        P.flush(block)
        p5.close()
    return nc


_NC_CACHE = [None]


def prep_in_maps(inputs):
    f32 = lambda a: np.ascontiguousarray(np.asarray(a, dtype=np.float32))
    x = f32(inputs["L0_attn_x"])
    shared = dict(
        wq=f32(inputs["L0_attn_Wq"]),
        wkv=f32(inputs["L0_attn_Wkv"]),
        wo=f32(inputs["L0_attn_Wo"]),
        g1t=np.ascontiguousarray(f32(inputs["L0_attn_ln1_g"]).reshape(8, 128).T),
        b1t=np.ascontiguousarray(f32(inputs["L0_attn_ln1_b"]).reshape(8, 128).T),
        g2r=np.ascontiguousarray(np.broadcast_to(f32(inputs["L0_attn_ln2_g"]), (128, D))),
        b2r=np.ascontiguousarray(np.broadcast_to(f32(inputs["L0_attn_ln2_b"]), (128, D))),
        w1t=np.ascontiguousarray(f32(inputs["L0_ffn_w1_w"]).T),
        w2t=np.ascontiguousarray(f32(inputs["L0_ffn_w2_w"]).T),
        h1t=np.ascontiguousarray(f32(inputs["L0_ffn_h1_w"]).T),
        h2t=np.ascontiguousarray(f32(inputs["L0_ffn_h2_w"]).T),
        w3t=np.ascontiguousarray(f32(inputs["L0_ffn_w3_w"]).T),
        s1t=np.ascontiguousarray(f32(inputs["L0_ffn_w1_s"]).reshape(32, 128).T),
        s2t=np.ascontiguousarray(f32(inputs["L0_ffn_w2_s"]).reshape(32, 128).T),
        h1st=np.ascontiguousarray(f32(inputs["L0_ffn_h1_s"]).reshape(32, 128).T),
        h2st=np.ascontiguousarray(f32(inputs["L0_ffn_h2_s"]).reshape(32, 128).T),
        s3r=np.ascontiguousarray(np.broadcast_to(f32(inputs["L0_ffn_w3_s"]), (128, D))),
    )
    return [dict(shared, x=np.ascontiguousarray(x[c])) for c in range(8)]


def kernel(**inputs):
    in_maps = prep_in_maps(inputs)
    if _NC_CACHE[0] is None:
        _NC_CACHE[0] = build_nc()
    res = run_bass_kernel_spmd(_NC_CACHE[0], in_maps, core_ids=list(range(8)))
    return np.stack([np.asarray(res.results[c]["y"]) for c in range(8)], axis=0)


# version 19 (final)
# speedup vs baseline: 1.0487x; 1.0487x over previous
import numpy as np
from contextlib import ExitStack

import concourse.bass as bass
import concourse.mybir as mybir
from concourse.bass_utils import run_bass_kernel_spmd

F32 = mybir.dt.float32
BF16 = mybir.dt.bfloat16
AF = mybir.ActivationFunctionType
ALU = mybir.AluOpType
AX = mybir.AxisListType

NT = 4096
D = 1024
H = 16
DH = 64
FF = 4096
NQG = 8
NKT = 32
TQ = 1024
ENGINES = ("sync", "gpsimd", "tensor", "vector", "scalar")
CSEM = {"sync": "dsy", "gpsimd": "dg", "tensor": "t", "vector": "v", "scalar": "s"}
DEBUG = False


class Buf:

    def __init__(self, ap):
        self.ap = ap
        self.w = None
        self.r = {}

    def rtoks(self):
        return list(self.r.values())


class Prog:
    def __init__(self):
        self.q = {e: [] for e in ENGINES}
        self.plan = {}
        self.handles = {}
        self.waited = {e: {} for e in ENGINES}
        self.floor = []

    def sem(self, nc, ctx, name):
        self.plan[name] = 0
        self.handles[name] = ctx.enter_context(nc.semaphore(name))

    def set_floor(self):
        self.floor = [(s, v, None, True) for s, v in self.plan.items() if v > 0]

    def _emit(self, eng, fn, waits, inc, dma):
        ws = tuple(w for w in waits if w is not None) + tuple(self.floor)
        tok = None
        if inc is not None:
            self.plan[inc] += 16 if dma else 1
            tok = (inc, self.plan[inc], eng, dma)
        self.q[eng].append((fn, ws, tok))
        return tok

    def op(self, eng, fn, reads=(), writes=(), extra=(), dma=False, inc=True):
        waits = []
        for b in reads:
            waits.append(b.w)
        for b in writes:
            waits.append(b.w)
            waits.extend(b.rtoks())
        waits.extend(extra)
        sem = (("dsy" if eng == "sync" else ("dsc" if eng == "scalar" else "dg")) if dma else
               ("g" if eng == "gpsimd" else CSEM[eng]))
        tok = self._emit(eng, fn, waits, sem if inc else None, dma)
        if tok is not None:
            for b in reads:
                b.r[tok[0]] = tok
            for b in writes:
                b.w = tok
                b.r = {}
        return tok

    def mm(self, bank, out_ap, pairs, reads=(), extra=(), start=True, stop=True):
        waits = ([bank.w] if not start else [bank.w] + bank.rtoks()) + \
            [b.w for b in reads] + list(extra)
        n = len(pairs)
        tok = None
        for i, (l, r) in enumerate(pairs):
            st = start and i == 0
            sp = stop and i == n - 1
            tok = self._emit(
                "tensor",
                (lambda E, o=out_ap, l=l, r=r, st=st, sp=sp:
                 E.matmul(o, l, r, start=st, stop=sp)),
                waits if i == 0 else (), "t" if i == n - 1 else None, False)
        bank.w = tok
        if stop:
            bank.r = {}
        for b in reads:
            b.r["t"] = tok
        return tok

    def flush(self, block):
        for eng in ENGINES:
            items = self.q[eng]
            if not items:
                continue
            self.q[eng] = []

            def body(E, items=items, eng=eng):
                waited = self.waited[eng]
                for fn, ws, tok in items:
                    for (sem, val, peng, isdma) in ws:
                        if peng == eng and not isdma and eng == "tensor":
                            continue
                        if waited.get(sem, 0) >= val:
                            continue
                        E.wait_ge(self.handles[sem], val)
                        waited[sem] = val
                    inst = fn(E)
                    if tok is not None:
                        inst.then_inc(self.handles[tok[0]], 16 if tok[3] else 1)
            getattr(block, eng)(body)


def build_nc():
    nc = bass.Bass("TRN2", target_bir_lowering=False)
    dt = lambda n, s, k: nc.dram_tensor(n, s, F32, kind=k).ap()
    x_d = dt("x", [NT, D], "ExternalInput")
    wq_d = dt("wq", [D, D], "ExternalInput")
    wkv_d = dt("wkv", [D, 2 * DH], "ExternalInput")
    wo_d = dt("wo", [D, D], "ExternalInput")
    g1_d = dt("g1t", [128, 8], "ExternalInput")
    b1_d = dt("b1t", [128, 8], "ExternalInput")
    g2_d = dt("g2r", [128, D], "ExternalInput")
    b2_d = dt("b2r", [128, D], "ExternalInput")
    w1_d = dt("w1t", [D, FF], "ExternalInput")
    w2_d = dt("w2t", [D, FF], "ExternalInput")
    h1_d = dt("h1t", [FF, FF], "ExternalInput")
    h2_d = dt("h2t", [FF, FF], "ExternalInput")
    w3_d = dt("w3t", [FF, D], "ExternalInput")
    s1_d = dt("s1t", [128, 32], "ExternalInput")
    s2_d = dt("s2t", [128, 32], "ExternalInput")
    h1s_d = dt("h1st", [128, 32], "ExternalInput")
    h2s_d = dt("h2st", [128, 32], "ExternalInput")
    s3_d = dt("s3r", [128, D], "ExternalInput")
    y_d = dt("y", [NT, D], "ExternalOutput")
    c_d = dt("c_scr", [NQG * 64, 128], "ExternalOutput" if DEBUG else "Internal")
    h1bf_d = nc.dram_tensor("h1bf_scr", [32, 128, 32 * 128], BF16, kind="Internal").ap()
    h2bf_d = nc.dram_tensor("h2bf_scr", [32, 128, 32 * 128], BF16, kind="Internal").ap()
    w3bf_d = nc.dram_tensor("w3bf_scr", [2, 8, 128, 2048], BF16, kind="Internal").ap()
    w12bf_d = nc.dram_tensor("w12bf_scr", [2, 16, 128, 2048], BF16, kind="Internal").ap()
    xp_d = dt("xp_scr", [NT, D], "ExternalOutput" if DEBUG else "Internal")
    if DEBUG:
        xn_dbg = dt("xn_dbg", [D, NT], "ExternalOutput")
        qh_dbg = dt("qh_dbg", [D, NT], "ExternalOutput")
        kx_dbg = dt("kx_dbg", [128, NT], "ExternalOutput")
        vx_dbg = dt("vx_dbg", [128, NKT * 65], "ExternalOutput")
        o_dbg = dt("o_dbg", [D, NT], "ExternalOutput")

    P = Prog()
    with ExitStack() as top:
        et = top.enter_context
        for sname in ("dsy", "dg", "dsc", "t", "v", "s", "g"):
            P.sem(nc, top, sname)

        SB = lambda n, s, d: Buf(et(nc.sbuf_tensor(n, s, d)))
        g1t = SB("g1t_sb", [128, 8], F32)
        b1t = SB("b1t_sb", [128, 8], F32)
        s1t = SB("s1t_sb", [128, 32], F32)
        s2t = SB("s2t_sb", [128, 32], F32)
        h1st = SB("h1st_sb", [128, 32], F32)
        h2st = SB("h2st_sb", [128, 32], F32)
        epsT = SB("epsT", [128, 1], F32)
        nhalfT = SB("nhalfT", [128, 1], F32)
        neg1T = SB("neg1T", [128, 1], F32)
        eighthT = SB("eighthT", [128, 1], F32)
        halfT = SB("halfT", [128, 1], F32)
        quartT = SB("quartT", [128, 1], F32)
        identb = SB("identb", [128, 128], BF16)
        identf = SB("identf", [128, 128], F32)
        ones64 = SB("ones64", [1, 64], F32)
        lnsets = []
        for i in range(2):
            lnsets.append(dict(
                st=SB(f"lnst{i}", [128, 24], F32),
                mv=SB(f"lnmv{i}", [128, 8], F32),
                aux=SB(f"lnax{i}", [128, 12], F32),
                lnv=SB(f"lnlv{i}", [128, 2], F32),
                rstd=SB(f"lnrs{i}", [128, 2], F32),
                negmr=SB(f"lnnm{i}", [128, 2], F32)))
        ln_rot = [0]
        psPa = et(nc.psum_tensor("psPa", [128, 1024], F32))
        psPb = et(nc.psum_tensor("psPb", [128, 1024], F32))
        pA0, pA1, pB0, pB1 = Buf(psPa), Buf(psPa), Buf(psPb), Buf(psPb)
        pb4 = Buf(et(nc.psum_tensor("pb4", [128, 512], F32)))
        pb5 = Buf(et(nc.psum_tensor("pb5", [128, 512], F32)))
        pb6 = Buf(et(nc.psum_tensor("pb6", [128, 512], F32)))
        pbt = [Buf(et(nc.psum_tensor("pbt0", [128, 1024], BF16)))]
        halves = [(pA0, psPa, 0), (pA1, psPa, 512), (pB0, psPb, 0), (pB1, psPb, 512)]
        cbuf = Buf(c_d)
        xpbuf = Buf(xp_d)

        block = et(nc.Block())

        def ln_pair(jobs, apply_eng="vector"):
            n = len(jobs)
            S = lnsets[ln_rot[0]]
            ln_rot[0] ^= 1
            st, mv, aux = S["st"], S["mv"], S["aux"]
            for s, (sb, _, _) in enumerate(jobs):
                P.op("vector", lambda E, a=st.ap, b=sb.ap, s=s: E.bn_stats(a[:, s * 12:s * 12 + 6], b[:, 0:512]),
                     reads=[sb], writes=[st])
                P.op("vector", lambda E, a=st.ap, b=sb.ap, s=s: E.bn_stats(a[:, s * 12 + 6:s * 12 + 12], b[:, 512:1024]),
                     reads=[sb], writes=[st])
            for s in range(n):
                P.op("vector", lambda E, a=mv.ap, b=st.ap, s=s: E.bn_aggr(a[:, s * 4:s * 4 + 2], b[:, s * 12:s * 12 + 6]),
                     reads=[st], writes=[mv])
                P.op("vector", lambda E, a=mv.ap, b=st.ap, s=s: E.bn_aggr(a[:, s * 4 + 2:s * 4 + 4], b[:, s * 12 + 6:s * 12 + 12]),
                     reads=[st], writes=[mv])
            mvv = lambda k0, k1: mv.ap[:, 0:4 * n].rearrange("p (s k) -> p s k", s=n)[:, :, k0:k1]
            P.op("vector", lambda E, a=aux.ap, n=n: E.tensor_add(
                a[:, 0:2 * n].rearrange("p (s k) -> p s k", s=n), mvv(0, 2), mvv(2, 4)),
                reads=[mv], writes=[aux])
            auxv = lambda c: aux.ap[:, 0:2 * n].rearrange("p (s k) -> p s k", s=n)[:, :, c:c + 1]
            P.op("vector", lambda E, a=aux.ap, n=n: E.tensor_sub(
                a[:, 4:4 + n].rearrange("p (s o) -> p s o", o=1), mvv(0, 1), mvv(2, 3)),
                reads=[mv], writes=[aux])
            P.op("vector", lambda E, a=aux.ap, n=n: E.tensor_mul(a[:, 6:6 + n], a[:, 4:4 + n], a[:, 4:4 + n]),
                 writes=[aux])
            P.op("vector", lambda E, a=aux.ap, n=n: E.tensor_scalar(
                a[:, 8:8 + n].rearrange("p (s o) -> p s o", o=1), auxv(1), halfT.ap[:, 0:1], None, ALU.mult),
                reads=[halfT], writes=[aux])
            P.op("vector", lambda E, a=aux.ap, n=n: E.scalar_tensor_tensor(
                a[:, 10:10 + n], a[:, 6:6 + n], quartT.ap[:, 0:1], a[:, 8:8 + n], ALU.mult, ALU.add),
                reads=[quartT], writes=[aux])
            P.op("scalar", lambda E, a=S["lnv"].ap, b=aux.ap, n=n: E.activation(
                a[:, 0:n], b[:, 10:10 + n], AF.Ln, bias=epsT.ap[:, :], scale=1.0),
                reads=[aux, epsT], writes=[S["lnv"]])
            P.op("scalar", lambda E, a=S["rstd"].ap, b=S["lnv"].ap, n=n: E.activation(
                a[:, 0:n], b[:, 0:n], AF.Exp, bias=0.0, scale=nhalfT.ap[:, :]),
                reads=[S["lnv"], nhalfT], writes=[S["rstd"]])
            P.op("vector", lambda E, a=S["negmr"].ap, b=S["rstd"].ap, n=n: E.scalar_tensor_tensor(
                a[:, 0:n], auxv(0).rearrange("p s o -> p (s o)") if False else aux.ap[:, 0:2 * n].rearrange("p (s k) -> p s k", s=n)[:, :, 0:1],
                nhalfT.ap[:, 0:1], b[:, 0:n], ALU.mult, ALU.mult),
                reads=[aux, S["rstd"], nhalfT], writes=[S["negmr"]])
            toks = []
            for s, (sb, dap, dbuf) in enumerate(jobs):
                toks.append(P.op(apply_eng, lambda E, a=dap, b=sb.ap, c=S["rstd"].ap, d=S["negmr"].ap, s=s: E.tensor_scalar(
                    a, b[:, :], c[:, s:s + 1], d[:, s:s + 1], ALU.mult, ALU.add),
                    reads=[sb, S["rstd"], S["negmr"]], writes=[dbuf]))
            return toks

        def ln_chain(src, dst_ap, dst_buf, extra_reads=()):
            return ln_pair([(src, dst_ap, dst_buf)])[0]

        for dst, src in ((g1t, g1_d), (b1t, b1_d),
                         (s1t, s1_d), (s2t, s2_d), (h1st, h1s_d), (h2st, h2s_d)):
            P.op("sync", lambda E, a=dst.ap, b=src: E.dma_start(a[:, :], b), writes=[dst], dma=True)
        for cb, val in ((epsT, 1e-5), (nhalfT, -0.5), (neg1T, -1.0), (eighthT, 0.125), (halfT, 0.5), (quartT, 0.25)):
            P.op("vector", lambda E, a=cb.ap, v=val: E.memset(a[:, :], v), writes=[cb])
        P.op("vector", lambda E: E.memset(ones64.ap[:, :], 1.0), writes=[ones64])
        P.op("gpsimd", lambda E: E.memset(identb.ap[:, :], 0.0), writes=[identb])
        P.op("gpsimd", lambda E: E.affine_select(out=identb.ap[:, :], in_=identb.ap[:, :], compare_op=ALU.not_equal,
                                                 fill=1.0, base=0, pattern=[[-1, 128]], channel_multiplier=1),
             writes=[identb])
        P.op("gpsimd", lambda E: E.memset(identf.ap[:, :], 0.0), writes=[identf])
        P.op("gpsimd", lambda E: E.affine_select(out=identf.ap[:, :], in_=identf.ap[:, :], compare_op=ALU.not_equal,
                                                 fill=1.0, base=0, pattern=[[-1, 128]], channel_multiplier=1),
             writes=[identf])

        pa = ExitStack()
        ep = pa.enter_context
        SA = lambda n, s, d: Buf(ep(nc.sbuf_tensor(n, s, d)))
        wq_sb = SA("wq_sb", [128, 8 * 1024], BF16)
        wkv_sb = SA("wkv_sb", [128, 8 * 128], BF16)
        qhat = [SA(f"qh{c}", [128, NT], BF16) for c in range(8)]
        kext = SA("kext", [128, NT], BF16)
        vext = SA("vext", [128, NKT * 65], BF16)
        ofm2 = [[SA(f"ofm{p}_{c}", [128, 512], BF16) for c in range(8)] for p in range(2)]
        lnA = [SA(f"lnA{i}", [128, D], F32) for i in range(2)]
        xnb = [SA(f"xnb{i}", [128, D], BF16) for i in range(2)]
        xnfm2 = [[SA(f"xnfm{p}_{c}", [128, 512], BF16) for c in range(8)] for p in range(2)]
        vtmp = SA("vtmp", [64, 512], BF16)
        qext = [SA(f"qext{i}", [128, 512], BF16) for i in range(2)]
        ptb = [SA(f"ptb{i}", [128, 1024], BF16) for i in range(3)]
        calx = [SA(f"calx{i}", [128, 4], F32) for i in range(2)]
        ctbx = [SA(f"ctbx{i}", [4, 128], F32) for i in range(2)]
        rrow = SA("rrow", [1, 512], F32)
        Rsb = SA("Rsb", [64, 512], F32)
        a_sb = [SA(f"a_sb{i}", [128, D], F32) for i in range(2)]
        t1 = [SA(f"t1_{i}", [128, D], F32) for i in range(2)]
        t2 = SA("t2", [128, D], F32)
        t3 = [SA(f"t3_{i}", [128, D], F32) for i in range(2)]
        xres = [SA(f"xr{i}", [128, D], F32) for i in range(2)]
        xtile = xres
        xp_sb = SA("xp_sb", [128, D], F32)
        hstg = SA("hstg", [128, 16 * 128], BF16)
        G2r = SA("g2r_sb", [128, D], F32)
        B2r = SA("b2r_sb", [128, D], F32)
        for dst, src in ((G2r, g2_d), (B2r, b2_d)):
            P.op("sync", lambda E, a=dst.ap, b=src: E.dma_start(a[:, :], b), writes=[dst], dma=True)
        h1bf = Buf(h1bf_d)
        h2bf = Buf(h2bf_d)

        w3bf = Buf(w3bf_d)
        w12bf = Buf(w12bf_d)
        precast_jobs = []
        for j in range(16):
            precast_jobs.append(("w12", w1_d, w12bf_d, w12bf, (0, j)))
            precast_jobs.append(("w12", w2_d, w12bf_d, w12bf, (1, j)))
        precast_jobs += [("w3", w3_d, w3bf_d, w3bf, j) for j in range(16)]
        precast_out = [None]

        def precast_step():
            if precast_out[0] is not None:
                kind, dstd, dbuf, j = precast_out[0]
                if kind == "h":
                    P.op("scalar", lambda E, d=dstd, j=j: E.dma_start(
                        d[j // 2][:, (j % 2) * 2048:(j % 2) * 2048 + 2048], hstg.ap[:, :]),
                        reads=[hstg], writes=[dbuf], dma=True)
                elif kind == "w12":
                    P.op("scalar", lambda E, d=dstd, wi=j[0], jj=j[1]: E.dma_start(
                        d[wi][jj], hstg.ap[:, :]),
                        reads=[hstg], writes=[dbuf], dma=True)
                else:
                    P.op("scalar", lambda E, d=dstd, j=j: E.dma_start(
                        d[j // 8][j % 8], hstg.ap[:, :]),
                        reads=[hstg], writes=[dbuf], dma=True)
                precast_out[0] = None
            if precast_jobs:
                kind, wd, dstd, dbuf, j = precast_jobs.pop(0)
                old_toks = [hstg.w] + hstg.rtoks()
                if kind == "w12":
                    tok = P.op("gpsimd", lambda E, wd=wd, jj=j[1]: E.dma_start(
                        hstg.ap[:, :].rearrange("p (kc n) -> p kc n", kc=8),
                        wd[:, jj * 256:(jj + 1) * 256].rearrange("(kc p) n -> p kc n", p=128)),
                        extra=old_toks, dma=True)
                elif kind == "h":
                    oc, half = j // 2, j % 2
                    for kq in range(2):
                        tok = P.op("gpsimd", lambda E, wd=wd, oc=oc, half=half, kq=kq: E.dma_start(
                            hstg.ap[:, kq * 8 * 128:(kq + 1) * 8 * 128].rearrange("p (kc n) -> p kc n", kc=8),
                            wd[(half * 2 + kq) * 1024:(half * 2 + kq + 1) * 1024,
                               oc * 128:(oc + 1) * 128].rearrange("(kc p) n -> p kc n", p=128)),
                            extra=old_toks if kq == 0 else (), dma=True)
                else:
                    douth, sub = j // 8, j % 8
                    tok = P.op("gpsimd", lambda E, wd=wd, douth=douth, sub=sub: E.dma_start(
                        hstg.ap[:, :].rearrange("p (kc n) -> p kc n", kc=4),
                        wd[sub * 512:(sub + 1) * 512,
                           douth * 512:(douth + 1) * 512].rearrange("(kc p) n -> p kc n", p=128)),
                        extra=old_toks, dma=True)
                hstg.w = tok
                hstg.r = {}
                precast_out[0] = (kind, dstd, dbuf, j)

        P.op("gpsimd", lambda E: E.dma_start(
            wq_sb.ap[:, :].rearrange("p (kc n) -> p kc n", kc=8),
            wq_d.rearrange("(kc p) n -> p kc n", p=128)), writes=[wq_sb], dma=True)
        P.op("gpsimd", lambda E: E.dma_start(
            wkv_sb.ap[:, :].rearrange("p (kc n) -> p kc n", kc=8),
            wkv_d.rearrange("(kc p) n -> p kc n", p=128)), writes=[wkv_sb], dma=True)
        P.op("gpsimd", lambda E: E.memset(vext.ap[:, :], 1.0), writes=[vext])
        P.op("gpsimd", lambda E: E.memset(kext.ap[64:65, :], 1.0), writes=[kext])

        def emit_proj(qg, part=None):
            xnfm = xnfm2[qg % 2]
            ochs = range(8) if part is None else (range(4) if part == 0 else range(4, 8))
            for och in ochs:
                bank, bt, boff = halves[(och % 2) * 2]
                pairs = [(wq_sb.ap[:, kc * 1024 + och * 128: kc * 1024 + (och + 1) * 128], xnfm[kc].ap[:, 0:512])
                         for kc in range(8)]
                P.mm(bank, bt[:, boff:boff + 512], pairs, reads=[wq_sb] + xnfm)
                P.op("scalar", lambda E, o=qhat[och].ap, b=bt, boff=boff, qg=qg: E.activation(
                    o[:, qg * 512:(qg + 1) * 512], b[:, boff:boff + 512], AF.Copy, bias=0.0, scale=eighthT.ap[:, :]),
                    reads=[bank, eighthT], writes=[qhat[och]])
            if part == 0:
                return
            bank = pb4
            pairs = [(wkv_sb.ap[:, kc * 128:(kc + 1) * 128], xnfm[kc].ap[:, 0:512]) for kc in range(8)]
            P.mm(bank, bank.ap[:, :], pairs, reads=[wkv_sb] + xnfm)
            P.op("vector", lambda E, b=bank.ap, qg=qg: E.tensor_copy(kext.ap[0:64, qg * 512:(qg + 1) * 512], b[0:64, :]),
                 reads=[bank], writes=[kext])
            P.op("vector", lambda E, b=bank.ap: E.tensor_copy(vtmp.ap[:, :], b[64:128, :]),
                 reads=[bank], writes=[vtmp])
            for j in range(4):
                kt = qg * 4 + j
                tb = pbt[0]
                P.op("tensor", lambda E, o=tb.ap, j=j: E.transpose(o[:, 0:64], vtmp.ap[:, j * 128:(j + 1) * 128], identb.ap[0:64, 0:64]),
                     reads=[vtmp, identb], writes=[tb])
                P.op("vector", lambda E, o=vext.ap, i=tb.ap, kt=kt: E.tensor_copy(o[:, kt * 65:kt * 65 + 64], i[:, 0:64]),
                     reads=[tb], writes=[vext])

        pend_proj = [None]
        for qg in range(NQG):
            xnfm = xnfm2[qg % 2]
            for qp in range(2):
                tiles = [qg * 4 + 2 * qp, qg * 4 + 2 * qp + 1]
                for i, qglob in enumerate(tiles):
                    P.op("sync", lambda E, a=xtile[i].ap, q=qglob: E.dma_start(a[:, :], x_d[q * 128:(q + 1) * 128, :]),
                         writes=[xtile[i]], dma=True)
                ln_pair([(xtile[i], lnA[i].ap[:, :], lnA[i]) for i in range(2)], apply_eng="gpsimd")
                if pend_proj[0] is not None:
                    emit_proj(pend_proj[0], part=qp)
                    if qp == 1:
                        pend_proj[0] = None
                precast_step()
                ln_pair([(lnA[i], xnb[i].ap[:, :], xnb[i]) for i in range(2)], apply_eng="gpsimd")
                for i, qglob in enumerate(tiles):
                    qt = qglob % 4
                    for c in range(8):
                        tb = pbt[0]
                        P.op("tensor", lambda E, o=tb.ap, i2=xnb[i].ap, c=c: E.transpose(
                            o[:, 0:128], i2[:, c * 128:(c + 1) * 128], identb.ap[:, :]),
                            reads=[xnb[i], identb], writes=[tb])
                        P.op("vector", lambda E, o=xnfm[c].ap, i2=tb.ap, c=c, qt=qt: E.tensor_scalar(
                            o[:, qt * 128:(qt + 1) * 128], i2[:, 0:128], g1t.ap[:, c:c + 1], b1t.ap[:, c:c + 1], ALU.mult, ALU.add),
                            reads=[tb, g1t, b1t], writes=[xnfm[c]])
            pend_proj[0] = qg
        emit_proj(pend_proj[0])
        pend_proj[0] = None

        if DEBUG:
            for c in range(8):
                P.op("gpsimd", lambda E, c=c: E.dma_start(qh_dbg[c * 128:(c + 1) * 128, :], qhat[c].ap[:, :]),
                     reads=[qhat[c]], dma=True)
            P.op("gpsimd", lambda E: E.dma_start(kx_dbg, kext.ap[:, :]), reads=[kext], dma=True)
            P.op("gpsimd", lambda E: E.dma_start(vx_dbg, vext.ap[:, :]), reads=[vext], dma=True)

        P.op("gpsimd", lambda E: E.dma_start(
            wq_sb.ap[:, :].rearrange("p (kc n) -> p kc n", kc=8),
            wo_d.rearrange("(kc p) n -> p kc n", p=128)), writes=[wq_sb], dma=True)

        pend_ep = [None]
        pend_p4 = []
        cur_p4 = [None]

        def emit_ep_part(part):
            if pend_ep[0] is None:
                return
            h, eqg, o2 = pend_ep[0]
            ch, base = h // 2, (h % 2) * 64
            if part == 1:
                P.op("vector", lambda E, b=o2.ap: E.reciprocal(rrow.ap[0:1, :], b[64:65, :]),
                     reads=[o2], writes=[rrow])
            elif part == 2:
                P.mm(pb6, pb6.ap[0:64, :], [(ones64.ap[0:1, :], rrow.ap[0:1, :])], reads=[ones64, rrow])
                P.op("vector", lambda E, b=pb6.ap: E.tensor_copy(Rsb.ap[:, :], b[0:64, 0:512]),
                     reads=[pb6], writes=[Rsb])
            else:
                P.op("vector", lambda E, b=o2.ap, ch=ch, base=base, p=eqg % 2: E.tensor_mul(
                    ofm2[p][ch].ap[base:base + 64, :], b[0:64, :], Rsb.ap[:, :]),
                    reads=[o2, Rsb], writes=[ofm2[eqg % 2][ch]])
                pend_ep[0] = None

        p4_idx = [0]
        p4_prev = [None]

        def p4_half(qg, qt, half):
            ofm = ofm2[qg % 2]
            qglob = qg * 4 + qt
            p = p4_idx[0] % 2
            if half == 0:
                xrb = xres[qglob % 2]
                P.op("sync", lambda E, a=xrb.ap, q=qglob: E.dma_start(a[:, :], x_d[q * 128:(q + 1) * 128, :]),
                     writes=[xrb], dma=True)
            pairs = [(ofm[c].ap[:, qt * 128:(qt + 1) * 128],
                      wq_sb.ap[:, c * 1024 + half * 512: c * 1024 + (half + 1) * 512]) for c in range(8)]
            P.mm(pb6, pb6.ap[:, :], pairs, reads=ofm + [wq_sb])
            P.op("vector", lambda E, b=pb6.ap, half=half, p=p: E.tensor_copy(
                a_sb[p].ap[:, half * 512:(half + 1) * 512], b[:, :]), reads=[pb6], writes=[a_sb[p]])

        def p4_residual(qg, qt, p):
            qglob = qg * 4 + qt
            xrb = xres[qglob % 2]
            P.op("vector", lambda E, b=xrb.ap, t=t1[p].ap: E.tensor_add(xp_sb.ap[:, :], t[:, :], b[:, :]),
                 reads=[t1[p], xrb], writes=[xp_sb])
            P.op("sync", lambda E, q=qglob: E.dma_start(xp_d[q * 128:(q + 1) * 128, :], xp_sb.ap[:, :]),
                 reads=[xp_sb], writes=[xpbuf], dma=True)

        def p4_tail(qg, qt):
            p = p4_idx[0] % 2
            jobs = [(a_sb[p], t1[p].ap[:, :], t1[p])]
            if p4_prev[0] is not None:
                pq, pt_, pp = p4_prev[0]
                jobs.append((t3[pp], t1[pp].ap[:, :], t1[pp]))
            ln_pair(jobs)
            P.op("vector", lambda E, p=p: E.tensor_mul(t2.ap[:, :], t1[p].ap[:, :], G2r.ap[:, :]),
                 reads=[t1[p], G2r], writes=[t2])
            P.op("vector", lambda E, p=p: E.tensor_add(t3[p].ap[:, :], t2.ap[:, :], B2r.ap[:, :]),
                 reads=[t2, B2r], writes=[t3[p]])
            if p4_prev[0] is not None:
                p4_residual(*p4_prev[0])
            p4_prev[0] = (qg, qt, p)
            p4_idx[0] += 1

        def p4_flush():
            if p4_prev[0] is None:
                return
            pq, pt_, pp = p4_prev[0]
            ln_chain(t3[pp], t1[pp].ap[:, :], t1[pp])
            p4_residual(pq, pt_, pp)
            p4_prev[0] = None

        def emit_p4_job(qg, qt):
            p4_half(qg, qt, 0)
            p4_half(qg, qt, 1)
            p4_tail(qg, qt)

        def head_prep_a(qg, h):
            qx = qext[h % 2]
            P.op("gpsimd", lambda E, o=qx.ap, ch=h // 2, base=(h % 2) * 64, qg=qg: E.tensor_copy(
                o[0:64, :], qhat[ch].ap[base:base + 64, qg * 512:(qg + 1) * 512]),
                reads=[qhat[h // 2]], writes=[qx])
            for qt in range(4):
                P.mm(pb6, pb6.ap[:, qt * 128:(qt + 1) * 128],
                     [(qx.ap[0:64, qt * 128:(qt + 1) * 128], kext.ap[0:64, 0:128])],
                     reads=[qx, kext], start=(qt == 0), stop=(qt == 3))
            P.op("vector", lambda E, b=pb6.ap, c=calx[h % 2].ap: E.tensor_reduce(
                c[:, 0:4], b[:, :].rearrange("p (q n) -> p q n", q=4), AX.X, ALU.max, negate=True),
                reads=[pb6], writes=[calx[h % 2]])

        def head_prep_b(qg, h):
            qx = qext[h % 2]
            P.op("tensor", lambda E, o=pb6.ap, c=calx[h % 2].ap: E.transpose(o[0:4, 0:128], c[:, 0:4], identf.ap[:, :]),
                 reads=[calx[h % 2], identf], writes=[pb6])
            P.op("vector", lambda E, o=ctbx[h % 2].ap, i=pb6.ap: E.tensor_copy(o[:, :], i[0:4, 0:128]),
                 reads=[pb6], writes=[ctbx[h % 2]])
            P.op("gpsimd", lambda E, qg=qg, h=h, c=ctbx[h % 2].ap: E.dma_start(
                c_d[qg * 64 + h * 4: qg * 64 + (h + 1) * 4, :], c[:, :]),
                reads=[ctbx[h % 2]], writes=[cbuf], dma=True)
            P.op("gpsimd", lambda E, o=qx.ap, qg=qg, h=h: E.dma_start(
                o[64:65, 0:512], c_d[qg * 64 + h * 4: qg * 64 + (h + 1) * 4, :].rearrange("a b -> (a b)")[None, :]),
                reads=[cbuf], writes=[qx], dma=True)

        for qg in range(NQG):
            for h in range(H):
                if h == 0:
                    head_prep_a(qg, 0)
                    head_prep_b(qg, 0)
                if h == 1 and qg == 0:
                    head_prep_a(qg, 1)
                    head_prep_b(qg, 1)
                qx = qext[h % 2]
                o2 = pb4 if h % 2 == 0 else pb5
                prev = [None] * 16
                for kp in range(NKT // 2):
                    h0, h1 = (pA0, pA1) if kp % 2 == 0 else (pB0, pB1)
                    pst = psPa if kp % 2 == 0 else psPb
                    kt0, kt1 = 2 * kp, 2 * kp + 1
                    P.mm(h0, pst[:, 0:512],
                         [(kext.ap[0:65, kt0 * 128:(kt0 + 1) * 128], qx.ap[0:65, :])],
                         reads=[kext, qx])
                    P.mm(h1, pst[:, 512:1024],
                         [(kext.ap[0:65, kt1 * 128:(kt1 + 1) * 128], qx.ap[0:65, :])],
                         reads=[kext, qx])
                    tp = ptb[kp % 3]
                    P.op("scalar", lambda E, o=tp.ap, b=pst: E.activation(
                        o[:, :], b[:, :], AF.Exp, bias=0.0, scale=1.0),
                        reads=[h0, h1], writes=[tp])
                    prev[kp] = tp
                    if kp == 1:
                        emit_ep_part(1)
                    elif kp == 2:
                        emit_ep_part(2)
                    elif kp == 3:
                        emit_ep_part(3)
                    elif kp == 4 and h % 4 == 3 and pend_p4:
                        cur_p4[0] = pend_p4.pop(0)
                        p4_half(*cur_p4[0], 0)
                    elif kp == 6 and cur_p4[0] is not None:
                        p4_half(*cur_p4[0], 1)
                    elif kp == 8 and cur_p4[0] is not None:
                        p4_tail(*cur_p4[0])
                        cur_p4[0] = None
                    elif kp == 10:
                        nh, nqg = (h + 1, qg) if h + 1 < H else (0, qg + 1)
                        if not (nh == 0 or nqg >= NQG):
                            head_prep_a(nqg, nh)
                    elif kp == 12:
                        nh, nqg = (h + 1, qg) if h + 1 < H else (0, qg + 1)
                        if not (nh == 0 or nqg >= NQG):
                            head_prep_b(nqg, nh)
                    elif kp == 14:
                        precast_step()
                    if kp > 1:
                        pk = kp - 2
                        tpp = prev[pk]
                        P.mm(o2, o2.ap[0:65, :],
                             [(vext.ap[:, (2 * pk) * 65:(2 * pk + 1) * 65], tpp.ap[:, 0:512])],
                             reads=[vext, tpp], start=(kp == 2), stop=False)
                        P.mm(o2, o2.ap[0:65, :],
                             [(vext.ap[:, (2 * pk + 1) * 65:(2 * pk + 2) * 65], tpp.ap[:, 512:1024])],
                             reads=[vext, tpp], start=False, stop=False)
                for pk in (NKT // 2 - 2, NKT // 2 - 1):
                    tpp = prev[pk]
                    P.mm(o2, o2.ap[0:65, :],
                         [(vext.ap[:, (2 * pk) * 65:(2 * pk + 1) * 65], tpp.ap[:, 0:512])],
                         reads=[vext, tpp], start=False, stop=False)
                    P.mm(o2, o2.ap[0:65, :],
                         [(vext.ap[:, (2 * pk + 1) * 65:(2 * pk + 2) * 65], tpp.ap[:, 512:1024])],
                         reads=[vext, tpp], start=False, stop=(pk == NKT // 2 - 1))
                emit_ep_part(1), emit_ep_part(2), emit_ep_part(3)
                pend_ep[0] = (h, qg, o2)
            pend_p4.extend((qg, qt) for qt in range(4))

        emit_ep_part(1), emit_ep_part(2), emit_ep_part(3)
        while pend_p4:
            emit_p4_job(*pend_p4.pop(0))
        p4_flush()
        while precast_jobs or precast_out[0] is not None:
            precast_step()

        P.flush(block)
        pa.close()
        P.set_floor()

        p5 = ExitStack()
        e5 = p5.enter_context
        S5 = lambda n, s, d: Buf(e5(nc.sbuf_tensor(n, s, d)))
        u1s = [S5(f"u1s{c}", [128, TQ], BF16) for c in range(32)]
        w1c = [S5(f"w1c{i}", [128, 8 * 256], BF16) for i in range(2)]
        w2c = [S5(f"w2c{i}", [128, 8 * 256], BF16) for i in range(2)]
        xp2 = [S5(f"xp2{i}", [128, D], F32) for i in range(2)]
        lnBb = [S5(f"lnBb{i}", [128, D], BF16) for i in range(2)]
        yln = None
        xr2 = [S5(f"xr2_{i}", [128, D], F32) for i in range(2)]
        lnBfm = [S5(f"lnBfm{c}", [128, TQ], BF16) for c in range(8)]
        lnBfmH = [[Buf(lnBfm[c].ap) for _ in range(2)] for c in range(8)]
        S3r = S5("s3r_sb", [128, D], F32)
        P.op("sync", lambda E: E.dma_start(S3r.ap[:, :], s3_d), writes=[S3r], dma=True)

        def emit_lnB_pair(Q, qp8):
            tiles = [Q * 8 + 2 * qp8, Q * 8 + 2 * qp8 + 1]
            for i, qglob in enumerate(tiles):
                P.op("sync", lambda E, a=xp2[i].ap, q=qglob: E.dma_start(a[:, :], xp_d[q * 128:(q + 1) * 128, :]),
                     reads=[xpbuf], writes=[xp2[i]], dma=True)
            ln_pair([(xp2[i], lnBb[i].ap[:, :], lnBb[i]) for i in range(2)])
            for i, qglob in enumerate(tiles):
                qt8 = qglob - Q * 8
                for c in range(8):
                    tb = pbt[0]
                    P.op("tensor", lambda E, o=tb.ap, i2=lnBb[i].ap, c=c: E.transpose(
                        o[:, 0:128], i2[:, c * 128:(c + 1) * 128], identb.ap[:, :]),
                        reads=[lnBb[i], identb], writes=[tb])
                    P.op("vector", lambda E, o=lnBfm[c].ap, i2=tb.ap, qt8=qt8: E.tensor_copy(
                        o[:, qt8 * 128:(qt8 + 1) * 128], i2[:, 0:128]),
                        reads=[tb], writes=[lnBfmH[c][qt8 // 4]])

        qgate = [[]]

        def load_w12(ocp, e1, e2):
            wb, wb2 = w1c[ocp % 2], w2c[ocp % 2]
            P.op(e1, lambda E, a=wb.ap, ocp=ocp: E.dma_start(a[:, :], w12bf_d[0][ocp]),
                 reads=[w12bf], writes=[wb], dma=True)
            P.op(e2, lambda E, a=wb2.ap, ocp=ocp: E.dma_start(a[:, :], w12bf_d[1][ocp]),
                 reads=[w12bf], writes=[wb2], dma=True)

        load_w12(0, "sync", "gpsimd")
        for qp8 in range(4):
            emit_lnB_pair(0, qp8)
        for Q in range(4):
            ch_ctx = ExitStack()
            eh = ch_ctx.enter_context
            Sh = lambda n, s, d: Buf(eh(nc.sbuf_tensor(f"{n}_q{Q}", s, d)))
            h1s = [Sh(f"h1s{c}", [128, TQ], BF16) for c in range(32)]
            h1c = [Sh(f"h1c{i}", [128, 32 * 128], BF16) for i in range(2)]
            for b in h1s + h1c:
                b.w = None
                b.r = {s_: t_ for s_, t_ in [(g[0], g) for g in qgate[0]]}

            for ocp in range(16):
                if ocp + 1 < 16:
                    load_w12(ocp + 1, "sync", "gpsimd")
                elif Q < 3:
                    load_w12(0, "sync", "sync")
                wb = w1c[ocp % 2]
                wb2 = w2c[ocp % 2]
                for ocw in range(2):
                    oc = 2 * ocp + ocw
                    for th in range(2):
                        bank, bt, boff = halves[th]
                        pairs = [(wb.ap[:, kc * 256 + ocw * 128: kc * 256 + ocw * 128 + 128],
                                  lnBfm[kc].ap[:, th * 512:(th + 1) * 512]) for kc in range(8)]
                        P.mm(bank, bt[:, boff:boff + 512], pairs, reads=[wb] + [lnBfmH[kc][th] for kc in range(8)])
                        P.op("scalar", lambda E, o=u1s[oc].ap, b=bt, boff=boff, th=th, oc=oc: E.activation(
                            o[:, th * 512:(th + 1) * 512], b[:, boff:boff + 512], AF.Silu, bias=0.0, scale=s1t.ap[:, oc:oc + 1]),
                            reads=[bank, s1t], writes=[u1s[oc]])
                    for th in range(2):
                        bank, bt, boff = halves[2 + th]
                        pairs = [(wb2.ap[:, kc * 256 + ocw * 128: kc * 256 + ocw * 128 + 128],
                                  lnBfm[kc].ap[:, th * 512:(th + 1) * 512]) for kc in range(8)]
                        P.mm(bank, bt[:, boff:boff + 512], pairs, reads=[wb2] + [lnBfmH[kc][th] for kc in range(8)])
                        P.op("vector", lambda E, o=u1s[oc].ap, b=bt, boff=boff, th=th: E.tensor_mul(
                            o[:, th * 512:(th + 1) * 512], b[:, boff:boff + 512], o[:, th * 512:(th + 1) * 512]),
                            reads=[bank, u1s[oc]], writes=[u1s[oc]])
                        P.op("vector", lambda E, o=u1s[oc].ap, th=th, oc=oc: E.tensor_scalar(
                            o[:, th * 512:(th + 1) * 512], o[:, th * 512:(th + 1) * 512], s2t.ap[:, oc:oc + 1], None, ALU.mult),
                            reads=[s2t, u1s[oc]], writes=[u1s[oc]])

            for (wd, sct, src, dst, pb_off) in ((h1_d, h1st, u1s, h1s, 0), (h2_d, h2st, h1s, u1s, 2)):
                for oc in range(32):
                    if Q < 3 and pb_off == 0 and oc in (8, 14, 20, 26):
                        emit_lnB_pair(Q + 1, (oc - 8) // 6)
                    wb = h1c[oc % 2]
                    old_toks = [wb.w] + wb.rtoks()
                    for kq in range(4):
                        tok = P.op("gpsimd", lambda E, a=wb.ap, oc=oc, kq=kq, wd=wd: E.dma_start(
                            a[:, kq * 8 * 128:(kq + 1) * 8 * 128].rearrange("p (kc n) -> p kc n", kc=8),
                            wd[kq * 1024:(kq + 1) * 1024, oc * 128:(oc + 1) * 128].rearrange("(kc p) n -> p kc n", p=128)),
                            extra=old_toks if kq == 0 else (), dma=True)
                    wb.w = tok
                    wb.r = {}
                    for th in range(2):
                        bank, bt, boff = halves[pb_off + th]
                        pairs = [(wb.ap[:, kc * 128:(kc + 1) * 128], src[kc].ap[:, th * 512:(th + 1) * 512])
                                 for kc in range(32)]
                        P.mm(bank, bt[:, boff:boff + 512], pairs, reads=[wb] + src)
                        P.op("scalar", lambda E, o=dst[oc].ap, b=bt, boff=boff, th=th, oc=oc, sct=sct: E.activation(
                            o[:, th * 512:(th + 1) * 512], b[:, boff:boff + 512], AF.Gelu, bias=0.0, scale=sct.ap[:, oc:oc + 1]),
                            reads=[bank, sct], writes=[dst[oc]])

            P.flush(block)
            ch_ctx.close()
            ygate = [("t", P.plan["t"], None, True), ("s", P.plan["s"], None, True)]

            cy_ctx = ExitStack()
            ey = cy_ctx.enter_context
            Sy = lambda n, s, d: Buf(ey(nc.sbuf_tensor(f"{n}_q{Q}", s, d)))
            w3h = Sy("w3h", [128, 32 * 512], BF16)
            w3hH = [Buf(w3h.ap) for _ in range(2)]
            y8 = [Sy(f"y8_{i}", [128, D], F32) for i in range(8)]
            for douth in range(2):
                for hh in range(2):
                    old_toks = [w3hH[hh].w] + w3hH[hh].rtoks() + (ygate if douth == 0 else [])
                    eng = "sync" if hh == 0 else "gpsimd"
                    for sub in range(4 * hh, 4 * hh + 4):
                        tok = P.op(eng, lambda E, douth=douth, sub=sub: E.dma_start(
                            w3h.ap[:, sub * 2048:(sub + 1) * 2048], w3bf_d[douth][sub]),
                            reads=[w3bf], extra=old_toks if sub == 4 * hh else (), dma=True)
                    w3hH[hh].w = tok
                    w3hH[hh].r = {}
                for qt8 in range(8):
                    bank, bt, boff = halves[(qt8 % 2) + (0 if douth == 0 else 2)]
                    for hh in range(2):
                        pairs = [(u1s[kc].ap[:, qt8 * 128:(qt8 + 1) * 128], w3h.ap[:, kc * 512:(kc + 1) * 512])
                                 for kc in range(16 * hh, 16 * hh + 16)]
                        P.mm(bank, bt[:, boff:boff + 512], pairs,
                             reads=[u1s[kc] for kc in range(16 * hh, 16 * hh + 16)] + [w3hH[hh]],
                             start=(hh == 0), stop=(hh == 1))
                    P.op("vector", lambda E, o=y8[qt8].ap, b=bt, boff=boff, douth=douth: E.scalar_tensor_tensor(
                        o[:, douth * 512:(douth + 1) * 512], b[:, boff:boff + 512], 1.0, S3r.ap[:, douth * 512:(douth + 1) * 512],
                        ALU.mult, ALU.mult),
                        reads=[bank, S3r], writes=[y8[qt8]],
                        extra=ygate if douth == 0 else ())
            for qp8 in range(4):
                tiles = [Q * 8 + 2 * qp8, Q * 8 + 2 * qp8 + 1]
                for i, qglob in enumerate(tiles):
                    P.op("sync", lambda E, a=xr2[i].ap, q=qglob: E.dma_start(a[:, :], xp_d[q * 128:(q + 1) * 128, :]),
                         reads=[xpbuf], writes=[xr2[i]], dma=True)
                ln_pair([(y8[t - Q * 8], y8[t - Q * 8].ap[:, :], y8[t - Q * 8]) for t in tiles])
                for i, qglob in enumerate(tiles):
                    yb = y8[qglob - Q * 8]
                    P.op("vector", lambda E, b=xr2[i].ap, y=yb.ap: E.tensor_add(b[:, :], y[:, :], b[:, :]),
                         reads=[yb], writes=[xr2[i]])
                    P.op("sync", lambda E, q=qglob, i=i: E.dma_start(y_d[q * 128:(q + 1) * 128, :], xr2[i].ap[:, :]),
                         reads=[xr2[i]], dma=True)
            P.flush(block)
            cy_ctx.close()
            qgate[0] = [("t", P.plan["t"], None, True), ("v", P.plan["v"], None, True),
                        ("s", P.plan["s"], None, True), ("dsy", P.plan["dsy"], None, True)]

        P._emit("sync", lambda E: E.nop(), [("dsy", P.plan["dsy"], None, True)], None, False)
        P.flush(block)
        p5.close()
    return nc


_NC_CACHE = [None]


def prep_in_maps(inputs):
    f32 = lambda a: np.ascontiguousarray(np.asarray(a, dtype=np.float32))
    x = f32(inputs["L0_attn_x"])
    shared = dict(
        wq=f32(inputs["L0_attn_Wq"]),
        wkv=f32(inputs["L0_attn_Wkv"]),
        wo=f32(inputs["L0_attn_Wo"]),
        g1t=np.ascontiguousarray(f32(inputs["L0_attn_ln1_g"]).reshape(8, 128).T),
        b1t=np.ascontiguousarray(f32(inputs["L0_attn_ln1_b"]).reshape(8, 128).T),
        g2r=np.ascontiguousarray(np.broadcast_to(f32(inputs["L0_attn_ln2_g"]), (128, D))),
        b2r=np.ascontiguousarray(np.broadcast_to(f32(inputs["L0_attn_ln2_b"]), (128, D))),
        w1t=np.ascontiguousarray(f32(inputs["L0_ffn_w1_w"]).T),
        w2t=np.ascontiguousarray(f32(inputs["L0_ffn_w2_w"]).T),
        h1t=np.ascontiguousarray(f32(inputs["L0_ffn_h1_w"]).T),
        h2t=np.ascontiguousarray(f32(inputs["L0_ffn_h2_w"]).T),
        w3t=np.ascontiguousarray(f32(inputs["L0_ffn_w3_w"]).T),
        s1t=np.ascontiguousarray(f32(inputs["L0_ffn_w1_s"]).reshape(32, 128).T),
        s2t=np.ascontiguousarray(f32(inputs["L0_ffn_w2_s"]).reshape(32, 128).T),
        h1st=np.ascontiguousarray(f32(inputs["L0_ffn_h1_s"]).reshape(32, 128).T),
        h2st=np.ascontiguousarray(f32(inputs["L0_ffn_h2_s"]).reshape(32, 128).T),
        s3r=np.ascontiguousarray(np.broadcast_to(f32(inputs["L0_ffn_w3_s"]), (128, D))),
    )
    return [dict(shared, x=np.ascontiguousarray(x[c])) for c in range(8)]


def kernel(**inputs):
    in_maps = prep_in_maps(inputs)
    if _NC_CACHE[0] is None:
        _NC_CACHE[0] = build_nc()
    res = run_bass_kernel_spmd(_NC_CACHE[0], in_maps, core_ids=list(range(8)))
    return np.stack([np.asarray(res.results[c]["y"]) for c in range(8)], axis=0)


# version 20
# speedup vs baseline: 1.0852x; 1.0348x over previous
import numpy as np
from contextlib import ExitStack

import concourse.bass as bass
import concourse.mybir as mybir
from concourse.bass_utils import run_bass_kernel_spmd

F32 = mybir.dt.float32
BF16 = mybir.dt.bfloat16
AF = mybir.ActivationFunctionType
ALU = mybir.AluOpType
AX = mybir.AxisListType

NT = 4096
D = 1024
H = 16
DH = 64
FF = 4096
NQG = 8
NKT = 32
TQ = 1024
ENGINES = ("sync", "gpsimd", "tensor", "vector", "scalar")
CSEM = {"sync": "dsy", "gpsimd": "dg", "tensor": "t", "vector": "v", "scalar": "s"}
DEBUG = False


class Buf:

    def __init__(self, ap):
        self.ap = ap
        self.w = None
        self.r = {}

    def rtoks(self):
        return list(self.r.values())


class Prog:
    def __init__(self):
        self.q = {e: [] for e in ENGINES}
        self.plan = {}
        self.handles = {}
        self.waited = {e: {} for e in ENGINES}
        self.floor = []

    def sem(self, nc, ctx, name):
        self.plan[name] = 0
        self.handles[name] = ctx.enter_context(nc.semaphore(name))

    def set_floor(self):
        self.floor = [(s, v, None, True) for s, v in self.plan.items() if v > 0]

    def _emit(self, eng, fn, waits, inc, dma):
        ws = tuple(w for w in waits if w is not None) + tuple(self.floor)
        tok = None
        if inc is not None:
            self.plan[inc] += 16 if dma else 1
            tok = (inc, self.plan[inc], eng, dma)
        self.q[eng].append((fn, ws, tok))
        return tok

    def op(self, eng, fn, reads=(), writes=(), extra=(), dma=False, inc=True):
        waits = []
        for b in reads:
            waits.append(b.w)
        for b in writes:
            waits.append(b.w)
            waits.extend(b.rtoks())
        waits.extend(extra)
        sem = (("dsy" if eng == "sync" else ("dsc" if eng == "scalar" else "dg")) if dma else
               ("g" if eng == "gpsimd" else CSEM[eng]))
        tok = self._emit(eng, fn, waits, sem if inc else None, dma)
        if tok is not None:
            for b in reads:
                b.r[tok[0]] = tok
            for b in writes:
                b.w = tok
                b.r = {}
        return tok

    def mm(self, bank, out_ap, pairs, reads=(), extra=(), start=True, stop=True):
        waits = ([bank.w] if not start else [bank.w] + bank.rtoks()) + \
            [b.w for b in reads] + list(extra)
        n = len(pairs)
        tok = None
        for i, (l, r) in enumerate(pairs):
            st = start and i == 0
            sp = stop and i == n - 1
            tok = self._emit(
                "tensor",
                (lambda E, o=out_ap, l=l, r=r, st=st, sp=sp:
                 E.matmul(o, l, r, start=st, stop=sp)),
                waits if i == 0 else (), "t" if i == n - 1 else None, False)
        bank.w = tok
        if stop:
            bank.r = {}
        for b in reads:
            b.r["t"] = tok
        return tok

    def flush(self, block):
        for eng in ENGINES:
            items = self.q[eng]
            if not items:
                continue
            self.q[eng] = []

            def body(E, items=items, eng=eng):
                waited = self.waited[eng]
                for fn, ws, tok in items:
                    for (sem, val, peng, isdma) in ws:
                        if peng == eng and not isdma and eng == "tensor":
                            continue
                        if waited.get(sem, 0) >= val:
                            continue
                        E.wait_ge(self.handles[sem], val)
                        waited[sem] = val
                    inst = fn(E)
                    if tok is not None:
                        inst.then_inc(self.handles[tok[0]], 16 if tok[3] else 1)
            getattr(block, eng)(body)


def build_nc():
    nc = bass.Bass("TRN2", target_bir_lowering=False)
    dt = lambda n, s, k: nc.dram_tensor(n, s, F32, kind=k).ap()
    x_d = dt("x", [NT, D], "ExternalInput")
    wq_d = dt("wq", [D, D], "ExternalInput")
    wkv_d = dt("wkv", [D, 2 * DH], "ExternalInput")
    wo_d = dt("wo", [D, D], "ExternalInput")
    g1_d = dt("g1t", [128, 8], "ExternalInput")
    b1_d = dt("b1t", [128, 8], "ExternalInput")
    g2_d = dt("g2r", [128, D], "ExternalInput")
    b2_d = dt("b2r", [128, D], "ExternalInput")
    w1_d = dt("w1t", [D, FF], "ExternalInput")
    w2_d = dt("w2t", [D, FF], "ExternalInput")
    h1_d = dt("h1t", [FF, FF], "ExternalInput")
    h2_d = dt("h2t", [FF, FF], "ExternalInput")
    w3_d = dt("w3t", [FF, D], "ExternalInput")
    s1_d = dt("s1t", [128, 32], "ExternalInput")
    s2_d = dt("s2t", [128, 32], "ExternalInput")
    h1s_d = dt("h1st", [128, 32], "ExternalInput")
    h2s_d = dt("h2st", [128, 32], "ExternalInput")
    s3_d = dt("s3r", [128, D], "ExternalInput")
    y_d = dt("y", [NT, D], "ExternalOutput")
    c_d = dt("c_scr", [NQG * 64, 128], "ExternalOutput" if DEBUG else "Internal")
    h1bf_d = nc.dram_tensor("h1bf_scr", [32, 128, 32 * 128], BF16, kind="Internal").ap()
    h2bf_d = nc.dram_tensor("h2bf_scr", [32, 128, 32 * 128], BF16, kind="Internal").ap()
    w3bf_d = nc.dram_tensor("w3bf_scr", [2, 8, 128, 2048], BF16, kind="Internal").ap()
    w12bf_d = nc.dram_tensor("w12bf_scr", [2, 16, 128, 2048], BF16, kind="Internal").ap()
    xp_d = dt("xp_scr", [NT, D], "ExternalOutput" if DEBUG else "Internal")
    if DEBUG:
        xn_dbg = dt("xn_dbg", [D, NT], "ExternalOutput")
        qh_dbg = dt("qh_dbg", [D, NT], "ExternalOutput")
        kx_dbg = dt("kx_dbg", [128, NT], "ExternalOutput")
        vx_dbg = dt("vx_dbg", [128, NKT * 65], "ExternalOutput")
        o_dbg = dt("o_dbg", [D, NT], "ExternalOutput")

    P = Prog()
    with ExitStack() as top:
        et = top.enter_context
        for sname in ("dsy", "dg", "dsc", "t", "v", "s", "g"):
            P.sem(nc, top, sname)

        SB = lambda n, s, d: Buf(et(nc.sbuf_tensor(n, s, d)))
        g1t = SB("g1t_sb", [128, 8], F32)
        b1t = SB("b1t_sb", [128, 8], F32)
        s1t = SB("s1t_sb", [128, 32], F32)
        s2t = SB("s2t_sb", [128, 32], F32)
        h1st = SB("h1st_sb", [128, 32], F32)
        h2st = SB("h2st_sb", [128, 32], F32)
        epsT = SB("epsT", [128, 1], F32)
        nhalfT = SB("nhalfT", [128, 1], F32)
        neg1T = SB("neg1T", [128, 1], F32)
        eighthT = SB("eighthT", [128, 1], F32)
        halfT = SB("halfT", [128, 1], F32)
        quartT = SB("quartT", [128, 1], F32)
        identb = SB("identb", [128, 128], BF16)
        identf = SB("identf", [128, 128], F32)
        ones64 = SB("ones64", [1, 64], F32)
        lnsets = []
        for i in range(2):
            lnsets.append(dict(
                st=SB(f"lnst{i}", [128, 24], F32),
                mv=SB(f"lnmv{i}", [128, 8], F32),
                aux=SB(f"lnax{i}", [128, 12], F32),
                lnv=SB(f"lnlv{i}", [128, 2], F32),
                rstd=SB(f"lnrs{i}", [128, 2], F32),
                negmr=SB(f"lnnm{i}", [128, 2], F32)))
        ln_rot = [0]
        psPa = et(nc.psum_tensor("psPa", [128, 1024], F32))
        psPb = et(nc.psum_tensor("psPb", [128, 1024], F32))
        pA0, pA1, pB0, pB1 = Buf(psPa), Buf(psPa), Buf(psPb), Buf(psPb)
        pb4 = Buf(et(nc.psum_tensor("pb4", [128, 512], F32)))
        pb5 = Buf(et(nc.psum_tensor("pb5", [128, 512], F32)))
        pb6 = Buf(et(nc.psum_tensor("pb6", [128, 512], F32)))
        pbt = [Buf(et(nc.psum_tensor("pbt0", [128, 1024], BF16)))]
        halves = [(pA0, psPa, 0), (pA1, psPa, 512), (pB0, psPb, 0), (pB1, psPb, 512)]
        cbuf = Buf(c_d)
        xpbuf = Buf(xp_d)

        block = et(nc.Block())

        def ln_pair(jobs, apply_eng="vector"):
            n = len(jobs)
            S = lnsets[ln_rot[0]]
            ln_rot[0] ^= 1
            st, mv, aux = S["st"], S["mv"], S["aux"]
            for s, (sb, _, _) in enumerate(jobs):
                P.op("vector", lambda E, a=st.ap, b=sb.ap, s=s: E.bn_stats(a[:, s * 12:s * 12 + 6], b[:, 0:512]),
                     reads=[sb], writes=[st])
                P.op("vector", lambda E, a=st.ap, b=sb.ap, s=s: E.bn_stats(a[:, s * 12 + 6:s * 12 + 12], b[:, 512:1024]),
                     reads=[sb], writes=[st])
            for s in range(n):
                P.op("vector", lambda E, a=mv.ap, b=st.ap, s=s: E.bn_aggr(a[:, s * 4:s * 4 + 2], b[:, s * 12:s * 12 + 6]),
                     reads=[st], writes=[mv])
                P.op("vector", lambda E, a=mv.ap, b=st.ap, s=s: E.bn_aggr(a[:, s * 4 + 2:s * 4 + 4], b[:, s * 12 + 6:s * 12 + 12]),
                     reads=[st], writes=[mv])
            mvv = lambda k0, k1: mv.ap[:, 0:4 * n].rearrange("p (s k) -> p s k", s=n)[:, :, k0:k1]
            P.op("vector", lambda E, a=aux.ap, n=n: E.tensor_add(
                a[:, 0:2 * n].rearrange("p (s k) -> p s k", s=n), mvv(0, 2), mvv(2, 4)),
                reads=[mv], writes=[aux])
            auxv = lambda c: aux.ap[:, 0:2 * n].rearrange("p (s k) -> p s k", s=n)[:, :, c:c + 1]
            P.op("vector", lambda E, a=aux.ap, n=n: E.tensor_sub(
                a[:, 4:4 + n].rearrange("p (s o) -> p s o", o=1), mvv(0, 1), mvv(2, 3)),
                reads=[mv], writes=[aux])
            P.op("vector", lambda E, a=aux.ap, n=n: E.tensor_mul(a[:, 6:6 + n], a[:, 4:4 + n], a[:, 4:4 + n]),
                 writes=[aux])
            P.op("vector", lambda E, a=aux.ap, n=n: E.tensor_scalar(
                a[:, 8:8 + n].rearrange("p (s o) -> p s o", o=1), auxv(1), halfT.ap[:, 0:1], None, ALU.mult),
                reads=[halfT], writes=[aux])
            P.op("vector", lambda E, a=aux.ap, n=n: E.scalar_tensor_tensor(
                a[:, 10:10 + n], a[:, 6:6 + n], quartT.ap[:, 0:1], a[:, 8:8 + n], ALU.mult, ALU.add),
                reads=[quartT], writes=[aux])
            P.op("scalar", lambda E, a=S["lnv"].ap, b=aux.ap, n=n: E.activation(
                a[:, 0:n], b[:, 10:10 + n], AF.Ln, bias=epsT.ap[:, :], scale=1.0),
                reads=[aux, epsT], writes=[S["lnv"]])
            P.op("scalar", lambda E, a=S["rstd"].ap, b=S["lnv"].ap, n=n: E.activation(
                a[:, 0:n], b[:, 0:n], AF.Exp, bias=0.0, scale=nhalfT.ap[:, :]),
                reads=[S["lnv"], nhalfT], writes=[S["rstd"]])
            P.op("vector", lambda E, a=S["negmr"].ap, b=S["rstd"].ap, n=n: E.scalar_tensor_tensor(
                a[:, 0:n], auxv(0).rearrange("p s o -> p (s o)") if False else aux.ap[:, 0:2 * n].rearrange("p (s k) -> p s k", s=n)[:, :, 0:1],
                nhalfT.ap[:, 0:1], b[:, 0:n], ALU.mult, ALU.mult),
                reads=[aux, S["rstd"], nhalfT], writes=[S["negmr"]])
            toks = []
            for s, (sb, dap, dbuf) in enumerate(jobs):
                toks.append(P.op(apply_eng, lambda E, a=dap, b=sb.ap, c=S["rstd"].ap, d=S["negmr"].ap, s=s: E.tensor_scalar(
                    a, b[:, :], c[:, s:s + 1], d[:, s:s + 1], ALU.mult, ALU.add),
                    reads=[sb, S["rstd"], S["negmr"]], writes=[dbuf]))
            return toks

        def ln_chain(src, dst_ap, dst_buf, extra_reads=()):
            return ln_pair([(src, dst_ap, dst_buf)])[0]

        for dst, src in ((g1t, g1_d), (b1t, b1_d),
                         (s1t, s1_d), (s2t, s2_d), (h1st, h1s_d), (h2st, h2s_d)):
            P.op("sync", lambda E, a=dst.ap, b=src: E.dma_start(a[:, :], b), writes=[dst], dma=True)
        for cb, val in ((epsT, 1e-5), (nhalfT, -0.5), (neg1T, -1.0), (eighthT, 0.125), (halfT, 0.5), (quartT, 0.25)):
            P.op("vector", lambda E, a=cb.ap, v=val: E.memset(a[:, :], v), writes=[cb])
        P.op("vector", lambda E: E.memset(ones64.ap[:, :], 1.0), writes=[ones64])
        P.op("gpsimd", lambda E: E.memset(identb.ap[:, :], 0.0), writes=[identb])
        P.op("gpsimd", lambda E: E.affine_select(out=identb.ap[:, :], in_=identb.ap[:, :], compare_op=ALU.not_equal,
                                                 fill=1.0, base=0, pattern=[[-1, 128]], channel_multiplier=1),
             writes=[identb])
        P.op("gpsimd", lambda E: E.memset(identf.ap[:, :], 0.0), writes=[identf])
        P.op("gpsimd", lambda E: E.affine_select(out=identf.ap[:, :], in_=identf.ap[:, :], compare_op=ALU.not_equal,
                                                 fill=1.0, base=0, pattern=[[-1, 128]], channel_multiplier=1),
             writes=[identf])

        pa = ExitStack()
        ep = pa.enter_context
        SA = lambda n, s, d: Buf(ep(nc.sbuf_tensor(n, s, d)))
        wq_sb = SA("wq_sb", [128, 8 * 1024], BF16)
        wkv_sb = SA("wkv_sb", [128, 8 * 128], BF16)
        qhat = [SA(f"qh{c}", [128, NT], BF16) for c in range(8)]
        kext = SA("kext", [128, NT], BF16)
        vext = SA("vext", [128, NKT * 65], BF16)
        ofm2 = [[SA(f"ofm{p}_{c}", [128, 512], BF16) for c in range(8)] for p in range(2)]
        lnA = [SA(f"lnA{i}", [128, D], F32) for i in range(2)]
        xnb = [SA(f"xnb{i}", [128, D], BF16) for i in range(2)]
        xnfm2 = [[SA(f"xnfm{p}_{c}", [128, 512], BF16) for c in range(8)] for p in range(2)]
        vtmp = SA("vtmp", [64, 512], BF16)
        qext = [SA(f"qext{i}", [128, 512], BF16) for i in range(2)]
        ptb = [SA(f"ptb{i}", [128, 1024], BF16) for i in range(3)]
        calx = [SA(f"calx{i}", [128, 4], F32) for i in range(2)]
        ctbx = [SA(f"ctbx{i}", [4, 128], F32) for i in range(2)]
        rrow = SA("rrow", [1, 512], F32)
        Rsb = SA("Rsb", [64, 512], F32)
        a_sb = [SA(f"a_sb{i}", [128, D], F32) for i in range(2)]
        t1 = [SA(f"t1_{i}", [128, D], F32) for i in range(2)]
        t2 = SA("t2", [128, D], F32)
        t3 = [SA(f"t3_{i}", [128, D], F32) for i in range(2)]
        xres = [SA(f"xr{i}", [128, D], F32) for i in range(2)]
        xtile = xres
        xp_sb = SA("xp_sb", [128, D], F32)
        hstg = SA("hstg", [128, 16 * 128], BF16)
        G2r = SA("g2r_sb", [128, D], F32)
        B2r = SA("b2r_sb", [128, D], F32)
        for dst, src in ((G2r, g2_d), (B2r, b2_d)):
            P.op("sync", lambda E, a=dst.ap, b=src: E.dma_start(a[:, :], b), writes=[dst], dma=True)
        h1bf = Buf(h1bf_d)
        h2bf = Buf(h2bf_d)

        w3bf = Buf(w3bf_d)
        w12bf = Buf(w12bf_d)
        precast_jobs = []
        for j in range(16):
            precast_jobs.append(("w12", w1_d, w12bf_d, w12bf, (0, j)))
            precast_jobs.append(("w12", w2_d, w12bf_d, w12bf, (1, j)))
        precast_jobs += [("w3", w3_d, w3bf_d, w3bf, j) for j in range(16)]
        precast_out = [None]

        def precast_step():
            if precast_out[0] is not None:
                kind, dstd, dbuf, j = precast_out[0]
                if kind == "h":
                    P.op("scalar", lambda E, d=dstd, j=j: E.dma_start(
                        d[j // 2][:, (j % 2) * 2048:(j % 2) * 2048 + 2048], hstg.ap[:, :]),
                        reads=[hstg], writes=[dbuf], dma=True)
                elif kind == "w12":
                    P.op("scalar", lambda E, d=dstd, wi=j[0], jj=j[1]: E.dma_start(
                        d[wi][jj], hstg.ap[:, :]),
                        reads=[hstg], writes=[dbuf], dma=True)
                else:
                    P.op("scalar", lambda E, d=dstd, j=j: E.dma_start(
                        d[j // 8][j % 8], hstg.ap[:, :]),
                        reads=[hstg], writes=[dbuf], dma=True)
                precast_out[0] = None
            if precast_jobs:
                kind, wd, dstd, dbuf, j = precast_jobs.pop(0)
                old_toks = [hstg.w] + hstg.rtoks()
                if kind == "w12":
                    tok = P.op("gpsimd", lambda E, wd=wd, jj=j[1]: E.dma_start(
                        hstg.ap[:, :].rearrange("p (kc n) -> p kc n", kc=8),
                        wd[:, jj * 256:(jj + 1) * 256].rearrange("(kc p) n -> p kc n", p=128)),
                        extra=old_toks, dma=True)
                elif kind == "h":
                    oc, half = j // 2, j % 2
                    for kq in range(2):
                        tok = P.op("gpsimd", lambda E, wd=wd, oc=oc, half=half, kq=kq: E.dma_start(
                            hstg.ap[:, kq * 8 * 128:(kq + 1) * 8 * 128].rearrange("p (kc n) -> p kc n", kc=8),
                            wd[(half * 2 + kq) * 1024:(half * 2 + kq + 1) * 1024,
                               oc * 128:(oc + 1) * 128].rearrange("(kc p) n -> p kc n", p=128)),
                            extra=old_toks if kq == 0 else (), dma=True)
                else:
                    douth, sub = j // 8, j % 8
                    tok = P.op("gpsimd", lambda E, wd=wd, douth=douth, sub=sub: E.dma_start(
                        hstg.ap[:, :].rearrange("p (kc n) -> p kc n", kc=4),
                        wd[sub * 512:(sub + 1) * 512,
                           douth * 512:(douth + 1) * 512].rearrange("(kc p) n -> p kc n", p=128)),
                        extra=old_toks, dma=True)
                hstg.w = tok
                hstg.r = {}
                precast_out[0] = (kind, dstd, dbuf, j)

        P.op("gpsimd", lambda E: E.dma_start(
            wq_sb.ap[:, :].rearrange("p (kc n) -> p kc n", kc=8),
            wq_d.rearrange("(kc p) n -> p kc n", p=128)), writes=[wq_sb], dma=True)
        P.op("gpsimd", lambda E: E.dma_start(
            wkv_sb.ap[:, :].rearrange("p (kc n) -> p kc n", kc=8),
            wkv_d.rearrange("(kc p) n -> p kc n", p=128)), writes=[wkv_sb], dma=True)
        P.op("gpsimd", lambda E: E.memset(vext.ap[:, :], 1.0), writes=[vext])
        P.op("gpsimd", lambda E: E.memset(kext.ap[64:65, :], 1.0), writes=[kext])

        def emit_proj(qg, part=None):
            xnfm = xnfm2[qg % 2]
            ochs = range(8) if part is None else (range(4) if part == 0 else range(4, 8))
            for och in ochs:
                bank, bt, boff = halves[(och % 2) * 2]
                pairs = [(wq_sb.ap[:, kc * 1024 + och * 128: kc * 1024 + (och + 1) * 128], xnfm[kc].ap[:, 0:512])
                         for kc in range(8)]
                P.mm(bank, bt[:, boff:boff + 512], pairs, reads=[wq_sb] + xnfm)
                P.op("scalar", lambda E, o=qhat[och].ap, b=bt, boff=boff, qg=qg: E.activation(
                    o[:, qg * 512:(qg + 1) * 512], b[:, boff:boff + 512], AF.Copy, bias=0.0, scale=eighthT.ap[:, :]),
                    reads=[bank, eighthT], writes=[qhat[och]])
            if part == 0:
                return
            bank = pb4
            pairs = [(wkv_sb.ap[:, kc * 128:(kc + 1) * 128], xnfm[kc].ap[:, 0:512]) for kc in range(8)]
            P.mm(bank, bank.ap[:, :], pairs, reads=[wkv_sb] + xnfm)
            P.op("vector", lambda E, b=bank.ap, qg=qg: E.tensor_copy(kext.ap[0:64, qg * 512:(qg + 1) * 512], b[0:64, :]),
                 reads=[bank], writes=[kext])
            P.op("vector", lambda E, b=bank.ap: E.tensor_copy(vtmp.ap[:, :], b[64:128, :]),
                 reads=[bank], writes=[vtmp])
            for j in range(4):
                kt = qg * 4 + j
                tb = pbt[0]
                P.op("tensor", lambda E, o=tb.ap, j=j: E.transpose(o[:, 0:64], vtmp.ap[:, j * 128:(j + 1) * 128], identb.ap[0:64, 0:64]),
                     reads=[vtmp, identb], writes=[tb])
                P.op("vector", lambda E, o=vext.ap, i=tb.ap, kt=kt: E.tensor_copy(o[:, kt * 65:kt * 65 + 64], i[:, 0:64]),
                     reads=[tb], writes=[vext])

        pend_proj = [None]
        for qg in range(NQG):
            xnfm = xnfm2[qg % 2]
            for qp in range(2):
                tiles = [qg * 4 + 2 * qp, qg * 4 + 2 * qp + 1]
                for i, qglob in enumerate(tiles):
                    P.op("sync", lambda E, a=xtile[i].ap, q=qglob: E.dma_start(a[:, :], x_d[q * 128:(q + 1) * 128, :]),
                         writes=[xtile[i]], dma=True)
                ln_pair([(xtile[i], lnA[i].ap[:, :], lnA[i]) for i in range(2)], apply_eng="gpsimd")
                if pend_proj[0] is not None:
                    emit_proj(pend_proj[0], part=qp)
                    if qp == 1:
                        pend_proj[0] = None
                precast_step()
                ln_pair([(lnA[i], xnb[i].ap[:, :], xnb[i]) for i in range(2)], apply_eng="gpsimd")
                for i, qglob in enumerate(tiles):
                    qt = qglob % 4
                    for c in range(8):
                        tb = pbt[0]
                        P.op("tensor", lambda E, o=tb.ap, i2=xnb[i].ap, c=c: E.transpose(
                            o[:, 0:128], i2[:, c * 128:(c + 1) * 128], identb.ap[:, :]),
                            reads=[xnb[i], identb], writes=[tb])
                        P.op("vector", lambda E, o=xnfm[c].ap, i2=tb.ap, c=c, qt=qt: E.tensor_scalar(
                            o[:, qt * 128:(qt + 1) * 128], i2[:, 0:128], g1t.ap[:, c:c + 1], b1t.ap[:, c:c + 1], ALU.mult, ALU.add),
                            reads=[tb, g1t, b1t], writes=[xnfm[c]])
            pend_proj[0] = qg
        emit_proj(pend_proj[0])
        pend_proj[0] = None

        if DEBUG:
            for c in range(8):
                P.op("gpsimd", lambda E, c=c: E.dma_start(qh_dbg[c * 128:(c + 1) * 128, :], qhat[c].ap[:, :]),
                     reads=[qhat[c]], dma=True)
            P.op("gpsimd", lambda E: E.dma_start(kx_dbg, kext.ap[:, :]), reads=[kext], dma=True)
            P.op("gpsimd", lambda E: E.dma_start(vx_dbg, vext.ap[:, :]), reads=[vext], dma=True)

        P.op("gpsimd", lambda E: E.dma_start(
            wq_sb.ap[:, :].rearrange("p (kc n) -> p kc n", kc=8),
            wo_d.rearrange("(kc p) n -> p kc n", p=128)), writes=[wq_sb], dma=True)

        pend_ep = [None]
        pend_p4 = []
        cur_p4 = [None]

        def emit_ep_part(part):
            if pend_ep[0] is None:
                return
            h, eqg, o2 = pend_ep[0]
            ch, base = h // 2, (h % 2) * 64
            if part == 1:
                P.op("vector", lambda E, b=o2.ap: E.reciprocal(rrow.ap[0:1, :], b[64:65, :]),
                     reads=[o2], writes=[rrow])
            elif part == 2:
                P.mm(pb6, pb6.ap[0:64, :], [(ones64.ap[0:1, :], rrow.ap[0:1, :])], reads=[ones64, rrow])
                P.op("vector", lambda E, b=pb6.ap: E.tensor_copy(Rsb.ap[:, :], b[0:64, 0:512]),
                     reads=[pb6], writes=[Rsb])
            else:
                P.op("vector", lambda E, b=o2.ap, ch=ch, base=base, p=eqg % 2: E.tensor_mul(
                    ofm2[p][ch].ap[base:base + 64, :], b[0:64, :], Rsb.ap[:, :]),
                    reads=[o2, Rsb], writes=[ofm2[eqg % 2][ch]])
                pend_ep[0] = None

        p4_idx = [0]
        p4_prev = [None]

        def p4_half(qg, qt, half):
            ofm = ofm2[qg % 2]
            qglob = qg * 4 + qt
            p = p4_idx[0] % 2
            if half == 0:
                xrb = xres[qglob % 2]
                P.op("sync", lambda E, a=xrb.ap, q=qglob: E.dma_start(a[:, :], x_d[q * 128:(q + 1) * 128, :]),
                     writes=[xrb], dma=True)
            pairs = [(ofm[c].ap[:, qt * 128:(qt + 1) * 128],
                      wq_sb.ap[:, c * 1024 + half * 512: c * 1024 + (half + 1) * 512]) for c in range(8)]
            P.mm(pb6, pb6.ap[:, :], pairs, reads=ofm + [wq_sb])
            P.op("vector", lambda E, b=pb6.ap, half=half, p=p: E.tensor_copy(
                a_sb[p].ap[:, half * 512:(half + 1) * 512], b[:, :]), reads=[pb6], writes=[a_sb[p]])

        def p4_residual(qg, qt, p):
            qglob = qg * 4 + qt
            xrb = xres[qglob % 2]
            P.op("vector", lambda E, b=xrb.ap, t=t1[p].ap: E.tensor_add(xp_sb.ap[:, :], t[:, :], b[:, :]),
                 reads=[t1[p], xrb], writes=[xp_sb])
            P.op("sync", lambda E, q=qglob: E.dma_start(xp_d[q * 128:(q + 1) * 128, :], xp_sb.ap[:, :]),
                 reads=[xp_sb], writes=[xpbuf], dma=True)

        def p4_tail(qg, qt):
            p = p4_idx[0] % 2
            jobs = [(a_sb[p], t1[p].ap[:, :], t1[p])]
            if p4_prev[0] is not None:
                pq, pt_, pp = p4_prev[0]
                jobs.append((t3[pp], t1[pp].ap[:, :], t1[pp]))
            ln_pair(jobs)
            P.op("vector", lambda E, p=p: E.tensor_mul(t2.ap[:, :], t1[p].ap[:, :], G2r.ap[:, :]),
                 reads=[t1[p], G2r], writes=[t2])
            P.op("vector", lambda E, p=p: E.tensor_add(t3[p].ap[:, :], t2.ap[:, :], B2r.ap[:, :]),
                 reads=[t2, B2r], writes=[t3[p]])
            if p4_prev[0] is not None:
                p4_residual(*p4_prev[0])
            p4_prev[0] = (qg, qt, p)
            p4_idx[0] += 1

        def p4_flush():
            if p4_prev[0] is None:
                return
            pq, pt_, pp = p4_prev[0]
            ln_chain(t3[pp], t1[pp].ap[:, :], t1[pp])
            p4_residual(pq, pt_, pp)
            p4_prev[0] = None

        def emit_p4_job(qg, qt):
            p4_half(qg, qt, 0)
            p4_half(qg, qt, 1)
            p4_tail(qg, qt)

        def head_prep_a(qg, h):
            qx = qext[h % 2]
            P.op("gpsimd", lambda E, o=qx.ap, ch=h // 2, base=(h % 2) * 64, qg=qg: E.tensor_copy(
                o[0:64, :], qhat[ch].ap[base:base + 64, qg * 512:(qg + 1) * 512]),
                reads=[qhat[h // 2]], writes=[qx])
            for qt in range(4):
                P.mm(pb6, pb6.ap[:, qt * 128:(qt + 1) * 128],
                     [(qx.ap[0:64, qt * 128:(qt + 1) * 128], kext.ap[0:64, 0:128])],
                     reads=[qx, kext], start=(qt == 0), stop=(qt == 3))
            P.op("vector", lambda E, b=pb6.ap, c=calx[h % 2].ap: E.tensor_reduce(
                c[:, 0:4], b[:, :].rearrange("p (q n) -> p q n", q=4), AX.X, ALU.max, negate=True),
                reads=[pb6], writes=[calx[h % 2]])

        def head_prep_b(qg, h):
            qx = qext[h % 2]
            P.op("tensor", lambda E, o=pb6.ap, c=calx[h % 2].ap: E.transpose(o[0:4, 0:128], c[:, 0:4], identf.ap[:, :]),
                 reads=[calx[h % 2], identf], writes=[pb6])
            P.op("vector", lambda E, o=ctbx[h % 2].ap, i=pb6.ap: E.tensor_copy(o[:, :], i[0:4, 0:128]),
                 reads=[pb6], writes=[ctbx[h % 2]])
            P.op("gpsimd", lambda E, qg=qg, h=h, c=ctbx[h % 2].ap: E.dma_start(
                c_d[qg * 64 + h * 4: qg * 64 + (h + 1) * 4, :], c[:, :]),
                reads=[ctbx[h % 2]], writes=[cbuf], dma=True)
            P.op("gpsimd", lambda E, o=qx.ap, qg=qg, h=h: E.dma_start(
                o[64:65, 0:512], c_d[qg * 64 + h * 4: qg * 64 + (h + 1) * 4, :].rearrange("a b -> (a b)")[None, :]),
                reads=[cbuf], writes=[qx], dma=True)

        for qg in range(NQG):
            for h in range(H):
                if h == 0:
                    head_prep_a(qg, 0)
                    head_prep_b(qg, 0)
                if h == 1 and qg == 0:
                    head_prep_a(qg, 1)
                    head_prep_b(qg, 1)
                qx = qext[h % 2]
                o2 = pb4 if h % 2 == 0 else pb5
                prev = [None] * 16
                for kp in range(NKT // 2):
                    h0, h1 = (pA0, pA1) if kp % 2 == 0 else (pB0, pB1)
                    pst = psPa if kp % 2 == 0 else psPb
                    kt0, kt1 = 2 * kp, 2 * kp + 1
                    P.mm(h0, pst[:, 0:512],
                         [(kext.ap[0:65, kt0 * 128:(kt0 + 1) * 128], qx.ap[0:65, :])],
                         reads=[kext, qx])
                    P.mm(h1, pst[:, 512:1024],
                         [(kext.ap[0:65, kt1 * 128:(kt1 + 1) * 128], qx.ap[0:65, :])],
                         reads=[kext, qx])
                    tp = ptb[kp % 3]
                    P.op("scalar", lambda E, o=tp.ap, b=pst: E.activation(
                        o[:, :], b[:, :], AF.Exp, bias=0.0, scale=1.0),
                        reads=[h0, h1], writes=[tp])
                    prev[kp] = tp
                    if kp == 1:
                        emit_ep_part(1)
                    elif kp == 2:
                        emit_ep_part(2)
                    elif kp == 3:
                        emit_ep_part(3)
                    elif kp == 4 and h % 4 == 3 and pend_p4:
                        cur_p4[0] = pend_p4.pop(0)
                        p4_half(*cur_p4[0], 0)
                    elif kp == 6 and cur_p4[0] is not None:
                        p4_half(*cur_p4[0], 1)
                    elif kp == 8 and cur_p4[0] is not None:
                        p4_tail(*cur_p4[0])
                        cur_p4[0] = None
                    elif kp == 10:
                        nh, nqg = (h + 1, qg) if h + 1 < H else (0, qg + 1)
                        if not (nh == 0 or nqg >= NQG):
                            head_prep_a(nqg, nh)
                    elif kp == 12:
                        nh, nqg = (h + 1, qg) if h + 1 < H else (0, qg + 1)
                        if not (nh == 0 or nqg >= NQG):
                            head_prep_b(nqg, nh)
                    elif kp == 14:
                        precast_step()
                    if kp > 1:
                        pk = kp - 2
                        tpp = prev[pk]
                        P.mm(o2, o2.ap[0:65, :],
                             [(vext.ap[:, (2 * pk) * 65:(2 * pk + 1) * 65], tpp.ap[:, 0:512])],
                             reads=[vext, tpp], start=(kp == 2), stop=False)
                        P.mm(o2, o2.ap[0:65, :],
                             [(vext.ap[:, (2 * pk + 1) * 65:(2 * pk + 2) * 65], tpp.ap[:, 512:1024])],
                             reads=[vext, tpp], start=False, stop=False)
                for pk in (NKT // 2 - 2, NKT // 2 - 1):
                    tpp = prev[pk]
                    P.mm(o2, o2.ap[0:65, :],
                         [(vext.ap[:, (2 * pk) * 65:(2 * pk + 1) * 65], tpp.ap[:, 0:512])],
                         reads=[vext, tpp], start=False, stop=False)
                    P.mm(o2, o2.ap[0:65, :],
                         [(vext.ap[:, (2 * pk + 1) * 65:(2 * pk + 2) * 65], tpp.ap[:, 512:1024])],
                         reads=[vext, tpp], start=False, stop=(pk == NKT // 2 - 1))
                emit_ep_part(1), emit_ep_part(2), emit_ep_part(3)
                pend_ep[0] = (h, qg, o2)
            pend_p4.extend((qg, qt) for qt in range(4))

        emit_ep_part(1), emit_ep_part(2), emit_ep_part(3)
        while pend_p4:
            emit_p4_job(*pend_p4.pop(0))
        p4_flush()
        while precast_jobs or precast_out[0] is not None:
            precast_step()

        P.flush(block)
        pa.close()
        P.set_floor()

        p5 = ExitStack()
        e5 = p5.enter_context
        S5 = lambda n, s, d: Buf(e5(nc.sbuf_tensor(n, s, d)))
        u1s = [S5(f"u1s{c}", [128, TQ], BF16) for c in range(32)]
        w1c = [S5(f"w1c{i}", [128, 8 * 256], BF16) for i in range(2)]
        w2c = [S5(f"w2c{i}", [128, 8 * 256], BF16) for i in range(2)]
        xp2 = [S5(f"xp2{i}", [128, D], F32) for i in range(2)]
        lnBb = [S5(f"lnBb{i}", [128, D], BF16) for i in range(2)]
        yln = None
        xr2 = [S5(f"xr2_{i}", [128, D], F32) for i in range(2)]
        lnBfm = [S5(f"lnBfm{c}", [128, TQ], BF16) for c in range(8)]
        lnBfmH = [[Buf(lnBfm[c].ap) for _ in range(2)] for c in range(8)]
        S3r = S5("s3r_sb", [128, D], F32)
        P.op("sync", lambda E: E.dma_start(S3r.ap[:, :], s3_d), writes=[S3r], dma=True)

        def emit_lnB_pair(Q, qp8):
            tiles = [Q * 8 + 2 * qp8, Q * 8 + 2 * qp8 + 1]
            for i, qglob in enumerate(tiles):
                P.op("sync", lambda E, a=xp2[i].ap, q=qglob: E.dma_start(a[:, :], xp_d[q * 128:(q + 1) * 128, :]),
                     reads=[xpbuf], writes=[xp2[i]], dma=True)
            ln_pair([(xp2[i], lnBb[i].ap[:, :], lnBb[i]) for i in range(2)])
            for i, qglob in enumerate(tiles):
                qt8 = qglob - Q * 8
                for c in range(8):
                    tb = pbt[0]
                    P.op("tensor", lambda E, o=tb.ap, i2=lnBb[i].ap, c=c: E.transpose(
                        o[:, 0:128], i2[:, c * 128:(c + 1) * 128], identb.ap[:, :]),
                        reads=[lnBb[i], identb], writes=[tb])
                    P.op("vector", lambda E, o=lnBfm[c].ap, i2=tb.ap, qt8=qt8: E.tensor_copy(
                        o[:, qt8 * 128:(qt8 + 1) * 128], i2[:, 0:128]),
                        reads=[tb], writes=[lnBfmH[c][qt8 // 4]])

        qgate = [[]]

        def load_w12(ocp, e1, e2):
            wb, wb2 = w1c[ocp % 2], w2c[ocp % 2]
            P.op(e1, lambda E, a=wb.ap, ocp=ocp: E.dma_start(a[:, :], w12bf_d[0][ocp]),
                 reads=[w12bf], writes=[wb], dma=True)
            P.op(e2, lambda E, a=wb2.ap, ocp=ocp: E.dma_start(a[:, :], w12bf_d[1][ocp]),
                 reads=[w12bf], writes=[wb2], dma=True)

        w3pre = [w1c[0], w1c[1], w2c[0], w2c[1]]
        load_w12(0, "sync", "gpsimd")
        for qp8 in range(4):
            emit_lnB_pair(0, qp8)
        for Q in range(4):
            ch_ctx = ExitStack()
            eh = ch_ctx.enter_context
            Sh = lambda n, s, d: Buf(eh(nc.sbuf_tensor(f"{n}_q{Q}", s, d)))
            h1s = [Sh(f"h1s{c}", [128, TQ], BF16) for c in range(32)]
            h1c = [Sh(f"h1c{i}", [128, 32 * 128], BF16) for i in range(2)]
            for b in h1s + h1c:
                b.w = None
                b.r = {s_: t_ for s_, t_ in [(g[0], g) for g in qgate[0]]}

            for ocp in range(16):
                if ocp + 1 < 16:
                    load_w12(ocp + 1, "sync", "gpsimd")
                wb = w1c[ocp % 2]
                wb2 = w2c[ocp % 2]
                for ocw in range(2):
                    oc = 2 * ocp + ocw
                    for th in range(2):
                        bank, bt, boff = halves[th]
                        pairs = [(wb.ap[:, kc * 256 + ocw * 128: kc * 256 + ocw * 128 + 128],
                                  lnBfm[kc].ap[:, th * 512:(th + 1) * 512]) for kc in range(8)]
                        P.mm(bank, bt[:, boff:boff + 512], pairs, reads=[wb] + [lnBfmH[kc][th] for kc in range(8)])
                        P.op("scalar", lambda E, o=u1s[oc].ap, b=bt, boff=boff, th=th, oc=oc: E.activation(
                            o[:, th * 512:(th + 1) * 512], b[:, boff:boff + 512], AF.Silu, bias=0.0, scale=s1t.ap[:, oc:oc + 1]),
                            reads=[bank, s1t], writes=[u1s[oc]])
                    for th in range(2):
                        bank, bt, boff = halves[2 + th]
                        pairs = [(wb2.ap[:, kc * 256 + ocw * 128: kc * 256 + ocw * 128 + 128],
                                  lnBfm[kc].ap[:, th * 512:(th + 1) * 512]) for kc in range(8)]
                        P.mm(bank, bt[:, boff:boff + 512], pairs, reads=[wb2] + [lnBfmH[kc][th] for kc in range(8)])
                        P.op("vector", lambda E, o=u1s[oc].ap, b=bt, boff=boff, th=th: E.tensor_mul(
                            o[:, th * 512:(th + 1) * 512], b[:, boff:boff + 512], o[:, th * 512:(th + 1) * 512]),
                            reads=[bank, u1s[oc]], writes=[u1s[oc]])
                        P.op("vector", lambda E, o=u1s[oc].ap, th=th, oc=oc: E.tensor_scalar(
                            o[:, th * 512:(th + 1) * 512], o[:, th * 512:(th + 1) * 512], s2t.ap[:, oc:oc + 1], None, ALU.mult),
                            reads=[s2t, u1s[oc]], writes=[u1s[oc]])

            for (wd, sct, src, dst, pb_off) in ((h1_d, h1st, u1s, h1s, 0), (h2_d, h2st, h1s, u1s, 2)):
                for oc in range(32):
                    if Q < 3 and pb_off == 0 and oc in (8, 14, 20, 26):
                        emit_lnB_pair(Q + 1, (oc - 8) // 6)
                    if pb_off == 2 and oc in (8, 14, 20, 26):
                        s = (oc - 8) // 6
                        P.op("sync", lambda E, a=w3pre[s].ap, s=s: E.dma_start(a[:, :], w3bf_d[0][s]),
                             reads=[w3bf], writes=[w3pre[s]], dma=True)
                    wb = h1c[oc % 2]
                    old_toks = [wb.w] + wb.rtoks()
                    for kq in range(4):
                        tok = P.op("gpsimd", lambda E, a=wb.ap, oc=oc, kq=kq, wd=wd: E.dma_start(
                            a[:, kq * 8 * 128:(kq + 1) * 8 * 128].rearrange("p (kc n) -> p kc n", kc=8),
                            wd[kq * 1024:(kq + 1) * 1024, oc * 128:(oc + 1) * 128].rearrange("(kc p) n -> p kc n", p=128)),
                            extra=old_toks if kq == 0 else (), dma=True)
                    wb.w = tok
                    wb.r = {}
                    for th in range(2):
                        bank, bt, boff = halves[pb_off + th]
                        pairs = [(wb.ap[:, kc * 128:(kc + 1) * 128], src[kc].ap[:, th * 512:(th + 1) * 512])
                                 for kc in range(32)]
                        P.mm(bank, bt[:, boff:boff + 512], pairs, reads=[wb] + src)
                        P.op("scalar", lambda E, o=dst[oc].ap, b=bt, boff=boff, th=th, oc=oc, sct=sct: E.activation(
                            o[:, th * 512:(th + 1) * 512], b[:, boff:boff + 512], AF.Gelu, bias=0.0, scale=sct.ap[:, oc:oc + 1]),
                            reads=[bank, sct], writes=[dst[oc]])

            P.flush(block)
            ch_ctx.close()
            ygate = [("t", P.plan["t"], None, True), ("s", P.plan["s"], None, True)]

            cy_ctx = ExitStack()
            ey = cy_ctx.enter_context
            Sy = lambda n, s, d: Buf(ey(nc.sbuf_tensor(f"{n}_q{Q}", s, d)))
            w3h = Sy("w3h", [128, 32 * 512], BF16)
            w3hH = [Buf(w3h.ap) for _ in range(2)]
            y8 = [Sy(f"y8_{i}", [128, D], F32) for i in range(8)]
            for douth in range(2):
                for hh in range(2):
                    if douth == 0 and hh == 0:
                        continue
                    old_toks = [w3hH[hh].w] + w3hH[hh].rtoks() + (ygate if w3hH[hh].w is None else [])
                    eng = "sync" if hh == 0 else "gpsimd"
                    for sub in range(4 * hh, 4 * hh + 4):
                        tok = P.op(eng, lambda E, douth=douth, sub=sub: E.dma_start(
                            w3h.ap[:, sub * 2048:(sub + 1) * 2048], w3bf_d[douth][sub]),
                            reads=[w3bf], extra=old_toks if sub == 4 * hh else (), dma=True)
                    w3hH[hh].w = tok
                    w3hH[hh].r = {}
                for qt8 in range(8):
                    bank, bt, boff = halves[(qt8 % 2) + (0 if douth == 0 else 2)]
                    for hh in range(2):
                        if douth == 0 and hh == 0:
                            pairs = [(u1s[kc].ap[:, qt8 * 128:(qt8 + 1) * 128],
                                      w3pre[kc // 4].ap[:, (kc % 4) * 512:(kc % 4 + 1) * 512])
                                     for kc in range(16)]
                            rds = [u1s[kc] for kc in range(16)] + w3pre
                        else:
                            pairs = [(u1s[kc].ap[:, qt8 * 128:(qt8 + 1) * 128], w3h.ap[:, kc * 512:(kc + 1) * 512])
                                     for kc in range(16 * hh, 16 * hh + 16)]
                            rds = [u1s[kc] for kc in range(16 * hh, 16 * hh + 16)] + [w3hH[hh]]
                        P.mm(bank, bt[:, boff:boff + 512], pairs,
                             reads=rds, start=(hh == 0), stop=(hh == 1))
                    if douth == 1 and qt8 == 0 and Q < 3:
                        load_w12(0, "sync", "sync")
                    P.op("vector", lambda E, o=y8[qt8].ap, b=bt, boff=boff, douth=douth: E.scalar_tensor_tensor(
                        o[:, douth * 512:(douth + 1) * 512], b[:, boff:boff + 512], 1.0, S3r.ap[:, douth * 512:(douth + 1) * 512],
                        ALU.mult, ALU.mult),
                        reads=[bank, S3r], writes=[y8[qt8]],
                        extra=ygate if douth == 0 else ())
            for qp8 in range(4):
                tiles = [Q * 8 + 2 * qp8, Q * 8 + 2 * qp8 + 1]
                for i, qglob in enumerate(tiles):
                    P.op("sync", lambda E, a=xr2[i].ap, q=qglob: E.dma_start(a[:, :], xp_d[q * 128:(q + 1) * 128, :]),
                         reads=[xpbuf], writes=[xr2[i]], dma=True)
                ln_pair([(y8[t - Q * 8], y8[t - Q * 8].ap[:, :], y8[t - Q * 8]) for t in tiles])
                for i, qglob in enumerate(tiles):
                    yb = y8[qglob - Q * 8]
                    P.op("vector", lambda E, b=xr2[i].ap, y=yb.ap: E.tensor_add(b[:, :], y[:, :], b[:, :]),
                         reads=[yb], writes=[xr2[i]])
                    P.op("sync", lambda E, q=qglob, i=i: E.dma_start(y_d[q * 128:(q + 1) * 128, :], xr2[i].ap[:, :]),
                         reads=[xr2[i]], dma=True)
            P.flush(block)
            cy_ctx.close()
            qgate[0] = [("t", P.plan["t"], None, True), ("v", P.plan["v"], None, True),
                        ("s", P.plan["s"], None, True), ("dsy", P.plan["dsy"], None, True)]

        P._emit("sync", lambda E: E.nop(), [("dsy", P.plan["dsy"], None, True)], None, False)
        P.flush(block)
        p5.close()
    return nc


_NC_CACHE = [None]


def prep_in_maps(inputs):
    f32 = lambda a: np.ascontiguousarray(np.asarray(a, dtype=np.float32))
    x = f32(inputs["L0_attn_x"])
    shared = dict(
        wq=f32(inputs["L0_attn_Wq"]),
        wkv=f32(inputs["L0_attn_Wkv"]),
        wo=f32(inputs["L0_attn_Wo"]),
        g1t=np.ascontiguousarray(f32(inputs["L0_attn_ln1_g"]).reshape(8, 128).T),
        b1t=np.ascontiguousarray(f32(inputs["L0_attn_ln1_b"]).reshape(8, 128).T),
        g2r=np.ascontiguousarray(np.broadcast_to(f32(inputs["L0_attn_ln2_g"]), (128, D))),
        b2r=np.ascontiguousarray(np.broadcast_to(f32(inputs["L0_attn_ln2_b"]), (128, D))),
        w1t=np.ascontiguousarray(f32(inputs["L0_ffn_w1_w"]).T),
        w2t=np.ascontiguousarray(f32(inputs["L0_ffn_w2_w"]).T),
        h1t=np.ascontiguousarray(f32(inputs["L0_ffn_h1_w"]).T),
        h2t=np.ascontiguousarray(f32(inputs["L0_ffn_h2_w"]).T),
        w3t=np.ascontiguousarray(f32(inputs["L0_ffn_w3_w"]).T),
        s1t=np.ascontiguousarray(f32(inputs["L0_ffn_w1_s"]).reshape(32, 128).T),
        s2t=np.ascontiguousarray(f32(inputs["L0_ffn_w2_s"]).reshape(32, 128).T),
        h1st=np.ascontiguousarray(f32(inputs["L0_ffn_h1_s"]).reshape(32, 128).T),
        h2st=np.ascontiguousarray(f32(inputs["L0_ffn_h2_s"]).reshape(32, 128).T),
        s3r=np.ascontiguousarray(np.broadcast_to(f32(inputs["L0_ffn_w3_s"]), (128, D))),
    )
    return [dict(shared, x=np.ascontiguousarray(x[c])) for c in range(8)]


def kernel(**inputs):
    in_maps = prep_in_maps(inputs)
    if _NC_CACHE[0] is None:
        _NC_CACHE[0] = build_nc()
    res = run_bass_kernel_spmd(_NC_CACHE[0], in_maps, core_ids=list(range(8)))
    return np.stack([np.asarray(res.results[c]["y"]) for c in range(8)], axis=0)
